# version 04 (final)
# baseline (speedup 1.0000x reference)
import os
os.environ.setdefault("GAUGE_NTFF_IGNORE_DMA", "1")
import numpy as np
import concourse.bass as bass
import concourse.mybir as mybir
from concourse.bass_utils import run_bass_kernel_spmd
from concourse.bass import IndirectOffsetOnAxis
import ml_dtypes

F32 = mybir.dt.float32
BF16 = mybir.dt.bfloat16
I32 = mybir.dt.int32
U8 = mybir.dt.uint8
U32 = mybir.dt.uint32
AF = mybir.ActivationFunctionType
ALU = mybir.AluOpType
AX = mybir.AxisListType

D = 1024
S = 2048
BC = 2
T = BC * S
PT = 128
NT = T // PT
NKC = D // PT
DFF = 4096

N_SUB = 8
DBG = False
SIM_MODE = False
BF = ml_dtypes.bfloat16
_ESZ = {F32: 4, BF16: 2, I32: 4, U8: 1, U32: 4}


def _ap_interval(ap):
    esz = _ESZ.get(ap.dtype, 4)
    lo = hi = int(ap.offset) if not hasattr(ap.offset, "name") else 0
    for (step, count) in list(ap.ap):
        d = int(step) * (int(count) - 1)
        if d >= 0:
            hi += d
        else:
            lo += d
    name = getattr(ap.tensor, "alloc_name", None) or ap.tensor.name
    return (name, lo * esz, (hi + 1) * esz)


class Prog:
    ENGINES = ("sync", "vector", "scalar", "tensor", "gpsimd")

    def __init__(self, nc):
        self.nc = nc
        self.ops = []
        self.eng_sems = {e: nc.alloc_semaphore(f"es_{e}") for e in self.ENGINES}
        self.eng_counts = {e: 0 for e in self.ENGINES}
        self.spaces = {}
        self.rings = {q: {"sems": [nc.alloc_semaphore(f"dq_{q}{i}") for i in range(4)],
                          "uses": [0, 0, 0, 0], "next": 0}
                      for q in ("sync", "gpsimd", "scalar")}
        self.collective_sem = nc.alloc_semaphore("cc_sem")
        self.cc_count = 0

    def _space(self, name):
        if name not in self.spaces:
            self.spaces[name] = {"w": [], "r": []}
        return self.spaces[name]

    def op(self, engine, fn, r=(), w=(), dma=False, tag="", cc=False):
        idx = len(self.ops)
        deps = set()
        rints = [_ap_interval(a) for a in r]
        wints = [_ap_interval(a) for a in w]
        for (name, lo, hi) in rints:
            sp = self._space(name)
            for (l2, h2, j) in sp["w"]:
                if l2 < hi and lo < h2:
                    deps.add(j)
        for (name, lo, hi) in wints:
            sp = self._space(name)
            for (l2, h2, j) in sp["w"]:
                if l2 < hi and lo < h2:
                    deps.add(j)
            for (l2, h2, j) in sp["r"]:
                if l2 < hi and lo < h2:
                    deps.add(j)
        for (name, lo, hi) in rints:
            self._space(name)["r"].append((lo, hi, idx))
        for (name, lo, hi) in wints:
            sp = self._space(name)
            sp["w"] = [(l2, h2, j) for (l2, h2, j) in sp["w"] if not (lo <= l2 and h2 <= hi)]
            sp["r"] = [(l2, h2, j) for (l2, h2, j) in sp["r"] if not (lo <= l2 and h2 <= hi)]
            sp["w"].append((lo, hi, idx))
        self.ops.append({"idx": idx, "eng": engine, "fn": fn, "deps": deps,
                         "dma": dma, "cc": cc, "tag": tag, "ms": None,
                         "needed": False, "slot": None})
        return idx

    def dma(self, queue, out, in_, tag="", extra_r=(), extra_w=()):
        def fn(e, o=out, i=in_):
            return e.dma_start(out=o, in_=i)
        return self.op(queue, fn, r=(in_,) + tuple(extra_r), w=(out,) + tuple(extra_w),
                       dma=True, tag=tag)

    def emit(self):
        nc = self.nc
        for rec in self.ops:
            for d in rec["deps"]:
                dep = self.ops[d]
                if (dep["eng"] == rec["eng"] == "tensor"
                        and not dep["dma"] and not rec["dma"] and not dep["cc"] and not rec["cc"]):
                    continue
                dep["needed"] = True
        for rec in self.ops:
            if rec["cc"]:
                self.cc_count += 1
                rec["ms"] = (self.collective_sem, self.cc_count, 1)
            elif rec["dma"]:
                ring = self.rings[rec["eng"]]
                s = ring["next"]
                ring["next"] = (s + 1) % len(ring["sems"])
                ring["uses"][s] += 1
                rec["slot"] = s
                rec["ms"] = (ring["sems"][s], 16 * ring["uses"][s], 16)
            elif rec["needed"]:
                self.eng_counts[rec["eng"]] += 1
                rec["ms"] = (self.eng_sems[rec["eng"]], self.eng_counts[rec["eng"]], 1)

        def dep_wait(rec, d):
            dep = self.ops[d]
            if (not dep["dma"] and not dep["cc"] and not rec["dma"] and not rec["cc"]
                    and dep["eng"] == rec["eng"] == "tensor"):
                return None
            if dep["ms"] is None:
                raise RuntimeError(f"dep {d} ({dep['tag']}) of {rec['idx']} ({rec['tag']}) lacks milestone")
            return dep["ms"][:2]

        with nc.Block() as block:
            for eng_name in self.ENGINES:
                ops = [rr for rr in self.ops if rr["eng"] == eng_name]
                if not ops:
                    continue
                deco = getattr(block, eng_name)

                def body(eng, ops=ops):
                    waited = {}
                    for rec in ops:
                        waits = []
                        for d in sorted(rec["deps"]):
                            wv = dep_wait(rec, d)
                            if wv is not None:
                                waits.append(wv)
                        if rec["dma"]:
                            ring = self.rings[rec["eng"]]
                            sem = ring["sems"][rec["slot"]]
                            prev = rec["ms"][1] - 16
                            if prev > 0:
                                waits.append((sem, prev))
                        for (sem, val) in waits:
                            k = id(sem)
                            if waited.get(k, -1) < val:
                                eng.wait_ge(sem, val)
                                waited[k] = val
                        inst = rec["fn"](eng)
                        if rec["ms"] is not None:
                            if inst is None:
                                inst = eng.nop()
                            inst.then_inc(rec["ms"][0], rec["ms"][2])
                deco(body)


def cols128(v):
    v = np.asarray(v, np.float32).ravel()
    k = v.size // PT
    return np.ascontiguousarray(v.reshape(k, PT).T)


def build_program():
    nc = bass.Bass("TRN2", target_bir_lowering=False)
    p = Prog(nc)
    ext = {}
    PREP = {}
    EAPS = {}

    def ein(name, shape, dtype=F32, prep=None):
        t = nc.dram_tensor(name, list(shape), dtype, kind="ExternalInput")
        ext[name] = (tuple(shape), dtype)
        EAPS[name] = t.ap()
        if prep is not None:
            PREP[name] = prep
        return t.ap()

    def nc_ap(name):
        return EAPS[name]

    x_in = ein("x", (T, D))
    out_t = nc.dram_tensor("out", [T, D], F32, kind="ExternalOutput").ap()
    ident = ein("ident", (PT, PT), BF16, prep=lambda inp: np.eye(PT, dtype=BF))
    identf = ein("identf", (PT, PT), F32, prep=lambda inp: np.eye(PT, dtype=np.float32))

    x_sb = nc.alloc_sbuf_tensor("x_sb", [PT, NT * D], F32)
    arena_a = nc.alloc_sbuf_tensor("arena_a", [PT, 16384], F32)
    arena_b = nc.alloc_sbuf_tensor("arena_b", [PT, 2048], F32)
    ident_sb = nc.alloc_sbuf_tensor("ident_sb", [PT, PT], BF16)
    identf_sb = nc.alloc_sbuf_tensor("identf_sb", [PT, PT], F32)
    vec_sb = nc.alloc_sbuf_tensor("vec_sb", [PT, 32], F32)
    r_sb = nc.alloc_sbuf_tensor("r_sb", [PT, NT], F32)
    nmr_sb = nc.alloc_sbuf_tensor("nmr_sb", [PT, NT], F32)
    stat_sb = nc.alloc_sbuf_tensor("stat_sb", [PT, 16], F32)
    xn_tile = nc.alloc_sbuf_tensor("xn_tile", [PT, D], BF16)
    ev_sb = nc.alloc_sbuf_tensor("ev_sb", [PT, 512], F32)
    wsm_sb = nc.alloc_sbuf_tensor("wsm_sb", [PT, 1024], BF16)
    snorm_sb = nc.alloc_sbuf_tensor("snorm_sb", [PT, 16], F32)
    stats2_sb = nc.alloc_sbuf_tensor("stats2_sb", [PT, 16], F32)
    s1q1_sb = nc.alloc_sbuf_tensor("s1q1_sb", [PT, 64], F32)
    ones_bf = nc.alloc_sbuf_tensor("ones_bf", [1, PT], BF16)
    prevcol_sb = nc.alloc_sbuf_tensor("prevcol_sb", [PT, 8], BF16)

    A_bf = arena_a.bitcast(BF16)
    B_bf = arena_b.bitcast(BF16)
    A_f = arena_a
    XT_bf = xn_tile
    sq_f = xn_tile.bitcast(F32)

    ps_f = [nc.alloc_psum_tensor(f"ps{i}", [PT, 512], F32) for i in range(6)]
    ps_bf = [nc.alloc_psum_tensor(f"psb{i}", [PT, 1024], BF16) for i in range(2)]

    p.op("vector", lambda e: e.memset(stat_sb[:, 14:15], 1e-5), w=[stat_sb[:, 14:15]], tag="c_eps5")
    p.op("vector", lambda e: e.memset(stat_sb[:, 15:16], 1e-6), w=[stat_sb[:, 15:16]], tag="c_eps6")
    EPS5 = stat_sb[:, 14:15]
    EPS6 = stat_sb[:, 15:16]
    p.op("vector", lambda e: e.memset(ones_bf[:], 1.0), w=[ones_bf[:]], tag="c_ones")

    def ln_tile(t, out_ap, eps=1e-5):
        xt = x_sb[:, t * D:(t + 1) * D]
        p.op("scalar", lambda e, xt=xt: e.activation(sq_f[:], xt[:, 0:512], AF.Square,
                                                     accum_out=stat_sb[:, 6:7]),
             r=[xt[:, 0:512]], w=[sq_f[:], stat_sb[:, 6:7]], tag=f"ln_sq{t}")
        p.op("scalar", lambda e, xt=xt: e.activation(sq_f[:], xt[:, 512:1024], AF.Square,
                                                     accum_out=stat_sb[:, 7:8]),
             r=[xt[:, 512:1024], sq_f[:]], w=[sq_f[:], stat_sb[:, 7:8]], tag=f"ln_sq2")
        p.op("vector", lambda e: e.tensor_add(stat_sb[:, 1:2], stat_sb[:, 6:7], stat_sb[:, 7:8]),
             r=[stat_sb[:, 6:7], stat_sb[:, 7:8]], w=[stat_sb[:, 1:2]], tag="ln_sqs")
        p.op("vector", lambda e, xt=xt: e.reduce_sum(stat_sb[:, 0:1], xt, axis=AX.X),
             r=[xt], w=[stat_sb[:, 0:1]], tag="ln_sum")
        p.op("vector", lambda e: e.tensor_scalar_mul(stat_sb[:, 2:3], stat_sb[:, 0:1], 1.0 / D),
             r=[stat_sb[:, 0:1]], w=[stat_sb[:, 2:3]], tag="ln1")
        p.op("vector", lambda e: e.tensor_mul(stat_sb[:, 3:4], stat_sb[:, 2:3], stat_sb[:, 2:3]),
             r=[stat_sb[:, 2:3]], w=[stat_sb[:, 3:4]], tag="ln2")
        p.op("vector", lambda e: e.scalar_tensor_tensor(stat_sb[:, 4:5], stat_sb[:, 1:2], 1.0 / D,
                                                        stat_sb[:, 3:4], ALU.mult, ALU.subtract),
             r=[stat_sb[:, 1:2], stat_sb[:, 3:4]], w=[stat_sb[:, 4:5]], tag="ln3")
        p.op("scalar", lambda e: e.activation(stat_sb[:, 5:6], stat_sb[:, 4:5],
                                              AF.Sqrt, bias=EPS5),
             r=[stat_sb[:, 4:5], EPS5], w=[stat_sb[:, 5:6]], tag="ln4")
        p.op("vector", lambda e, t=t: e.reciprocal(r_sb[:, t:t + 1], stat_sb[:, 5:6]),
             r=[stat_sb[:, 5:6]], w=[r_sb[:, t:t + 1]], tag="ln5")
        p.op("vector", lambda e, t=t: e.scalar_tensor_tensor(nmr_sb[:, t:t + 1], stat_sb[:, 2:3],
                                                             -1.0, r_sb[:, t:t + 1], ALU.mult, ALU.mult),
             r=[stat_sb[:, 2:3], r_sb[:, t:t + 1]], w=[nmr_sb[:, t:t + 1]], tag="ln6")
        p.op("scalar", lambda e, xt=xt, t=t, o=out_ap: e.activation(
                o, xt, AF.Identity, bias=nmr_sb[:, t:t + 1], scale=r_sb[:, t:t + 1]),
             r=[xt, nmr_sb[:, t:t + 1], r_sb[:, t:t + 1]], w=[out_ap], tag=f"ln_ap{t}")

    def transpose_blocks(src_ap, dst3_ap, bank_i, n, bf=True, tag="tr"):
        bank = ps_bf[bank_i] if bf else ps_f[bank_i]
        for c in range(n):
            p.op("tensor", lambda e, c=c, bank=bank: e.transpose(
                    bank[:, c * PT:(c + 1) * PT], src_ap[:, c * PT:(c + 1) * PT], ident_sb[:]),
                 r=[src_ap[:, c * PT:(c + 1) * PT], ident_sb[:]],
                 w=[bank[:, c * PT:(c + 1) * PT]], tag=tag)
        p.op("vector", lambda e, bank=bank: e.tensor_copy(
                dst3_ap, bank[:, 0:n * PT].rearrange("p (c m) -> p c m", c=n)),
             r=[bank[:, 0:n * PT]], w=[dst3_ap], tag=tag + "_ev")

    p.dma("sync", ident_sb[:], ident, tag="ld_ident")
    p.dma("sync", identf_sb[:], identf, tag="ld_identf")
    for t in range(NT):
        p.dma("sync", x_sb[:, t * D:(t + 1) * D], x_in[t * PT:(t + 1) * PT, :], tag="ld_x")

    def sublayer1():
        wg = ein("a_Wg", (D, D), prep=lambda i: np.ascontiguousarray(i["L0_attn_a_Wg"], np.float32))
        wc = ein("a_Wc", (D, D), prep=lambda i: np.ascontiguousarray(i["L0_attn_a_Wc"], np.float32))
        w1 = ein("a_W1", (D, DFF), prep=lambda i: np.ascontiguousarray(i["L0_attn_a_W1"], np.float32))
        w3 = ein("a_W3", (D, DFF), prep=lambda i: np.ascontiguousarray(i["L0_attn_a_W3"], np.float32))
        w2 = ein("a_W2", (DFF, D), prep=lambda i: np.ascontiguousarray(i["L0_attn_a_W2"], np.float32))
        a_vec = ein("a_vec", (PT, 32), prep=lambda i: np.concatenate([
            cols128(i["L0_attn_a_bg"]), cols128(i["L0_attn_a_bc"]),
            cols128(i["L0_attn_a_n1_w"]), cols128(i["L0_attn_a_n2_w"])], axis=1))
        wg_scr = nc.dram_tensor("wg_scr", [NKC * NKC * PT, PT], BF16).ap()
        wc_scr = nc.dram_tensor("wc_scr", [NKC * NKC * PT, PT], BF16).ap()
        w1_scr = nc.dram_tensor("w1_scr", [32 * NKC * PT, PT], BF16).ap()
        w3_scr = nc.dram_tensor("w3_scr", [32 * NKC * PT, PT], BF16).ap()
        w2_scr = nc.dram_tensor("w2_scr", [DFF, D], BF16).ap()
        h_scr = nc.dram_tensor("h_scr", [T, D], F32).ap()

        p.dma("sync", vec_sb[:, 0:32], a_vec, tag="a_vec")

        stage = B_bf[:, 0:4096]
        def prep_tiled(scr, src, n_in, n_out_tiles, fold_col):
            for kc in range(n_in):
                st = stage[:, 0:n_out_tiles * PT]
                p.dma("gpsimd", st, src[kc * PT:(kc + 1) * PT, :], tag="wcast")
                if fold_col is not None:
                    p.op("vector", lambda e, st=st, kc=kc, fc=fold_col: e.tensor_scalar(
                            st, st, vec_sb[:, fc + kc:fc + kc + 1], None, ALU.mult),
                         r=[st, vec_sb[:, fold_col + kc:fold_col + kc + 1]], w=[st], tag="wfold")
                dst = scr.rearrange("(o k p) c -> k p o c", o=n_out_tiles, k=n_in, p=PT)[kc]
                p.dma("sync", dst, st.rearrange("p (o c) -> p o c", o=n_out_tiles), tag="wst")

        prep_tiled(wg_scr, wg, NKC, NKC, 16)
        prep_tiled(wc_scr, wc, NKC, NKC, 16)
        prep_tiled(w1_scr, w1, NKC, 32, 24)
        prep_tiled(w3_scr, w3, NKC, 32, 24)
        p.dma("gpsimd", w2_scr[:, :], w2[:, :], tag="w2cast")

        for t in range(NT):
            ln_tile(t, XT_bf[:])
            fm_ap = A_bf[:].rearrange("p (c j) -> p c j", c=NKC)[:, :, t * PT:(t + 1) * PT]
            transpose_blocks(XT_bf[:], fm_ap, t % 2, NKC, bf=True, tag=f"xtr{t}")

        xn_fm = A_bf
        if DBG:
            dbg_xn = nc.dram_tensor("dbg_xn", [PT, 8 * T], BF16, kind="ExternalOutput").ap()
            p.dma("gpsimd", dbg_xn[:, :], A_bf[:, :], tag="dbg_xn")

        HS = 1024
        gat = arena_b[:, 0:1024]
        can = arena_b[:, 1024:2048]
        tmp = ev_sb[:, 0:512]
        wgoc = XT_bf
        wcoc = wsm_sb
        hlast = stat_sb[:, 12:13]
        for oc in range(NKC):
            for (scr, buf) in ((wg_scr, wgoc), (wc_scr, wcoc)):
                srcw = scr.rearrange("(o k p) c -> p o k c", o=NKC, k=NKC, p=PT)[:, oc]
                p.dma("sync", buf[:].rearrange("p (k c) -> p k c", k=NKC), srcw, tag=f"wgld{oc}")
            for b in range(BC):
                for sh in range(2):
                    base = b * S + sh * HS
                    for (buf, dst_sb, fn_act, bias_col) in (
                            (wgoc, gat, AF.Sigmoid, 0), (wcoc, can, AF.Tanh, 8)):
                        for ts in range(2):
                            bank = ps_f[ts % 2]
                            for kc in range(NKC):
                                rhs = xn_fm[:, kc * T + base + ts * 512: kc * T + base + (ts + 1) * 512]
                                p.op("tensor", lambda e, kc=kc, buf=buf, rhs=rhs, bank=bank: e.matmul(
                                        bank[:], buf[:, kc * PT:(kc + 1) * PT], rhs,
                                        start=(kc == 0), stop=(kc == NKC - 1)),
                                     r=[buf[:, kc * PT:(kc + 1) * PT], rhs], w=[bank[:]], tag="mm_gc")
                            p.op("scalar", lambda e, ts=ts, dst_sb=dst_sb, fn_act=fn_act,
                                           bias_col=bias_col, oc=oc, bank=bank: e.activation(
                                    dst_sb[:, ts * 512:(ts + 1) * 512], bank[:], fn_act,
                                    bias=vec_sb[:, bias_col + oc:bias_col + oc + 1]),
                                 r=[bank[:], vec_sb[:, bias_col + oc:bias_col + oc + 1]],
                                 w=[dst_sb[:, ts * 512:(ts + 1) * 512]], tag="act_gc")
                    for tsc in range(2):
                        sl = slice(tsc * 512, (tsc + 1) * 512)
                        p.op("vector", lambda e, sl=sl: e.tensor_mul(tmp, gat[:, sl], can[:, sl]),
                             r=[gat[:, sl], can[:, sl]], w=[tmp], tag="bmul")
                        p.op("vector", lambda e, sl=sl: e.tensor_sub(can[:, sl], can[:, sl], tmp),
                             r=[can[:, sl], tmp], w=[can[:, sl]], tag="bsub")
                    init = 0.0 if sh == 0 else hlast
                    if sh == 0:
                        p.op("vector", lambda e: e.tensor_tensor_scan(can[:], gat[:], can[:], 0.0,
                                                                      ALU.mult, ALU.add),
                             r=[gat[:], can[:]], w=[can[:]], tag="scan")
                    else:
                        p.op("vector", lambda e: e.tensor_tensor_scan(can[:], gat[:], can[:], hlast,
                                                                      ALU.mult, ALU.add),
                             r=[gat[:], can[:], hlast], w=[can[:]], tag="scan")
                    p.op("vector", lambda e: e.tensor_copy(hlast, can[:, HS - 1:HS]),
                         r=[can[:, HS - 1:HS]], w=[hlast], tag="hlast")
                    for g4 in range(2):
                        bank = ps_f[2 + g4 % 2]
                        for c4 in range(4):
                            srch = can[:, (g4 * 4 + c4) * PT:(g4 * 4 + c4 + 1) * PT]
                            p.op("tensor", lambda e, c4=c4, srch=srch, bank=bank: e.transpose(
                                    bank[:, c4 * PT:(c4 + 1) * PT], srch, identf_sb[:]),
                                 r=[srch, identf_sb[:]], w=[bank[:, c4 * PT:(c4 + 1) * PT]], tag="htr")
                        p.op("vector", lambda e, bank=bank: e.tensor_copy(ev_sb[:], bank[:]),
                             r=[bank[:]], w=[ev_sb[:]], tag="hev")
                        dsth = h_scr[base + g4 * 512: base + (g4 + 1) * 512,
                                     oc * PT:(oc + 1) * PT].rearrange("(q p) c -> p q c", p=PT)
                        p.dma("gpsimd", dsth, ev_sb[:].rearrange("p (q c) -> p q c", q=4), tag="hst")

        if DBG:
            dbg_h = nc.dram_tensor("dbg_h", [T, D], F32, kind="ExternalOutput").ap()
            p.dma("gpsimd", dbg_h[:, :], h_scr[:, :], tag="dbg_h")
        STT = 256
        NST = T // STT
        hh = A_bf[:, 0:32 * STT]
        f_fm = A_bf[:, 32 * STT:32 * STT + NKC * STT]
        w2res = A_bf[:, 40 * STT:40 * STT + 32 * 512]
        fb = A_bf[:, 40 * STT + 32 * 512:40 * STT + 32 * 512 + D]
        ht = A_f[:, 13824:14848]
        y2 = A_f[:, 14848:15872]
        w1buf = B_bf[:, 0:1024]
        w3buf = B_bf[:, 1024:2048]
        for st_i in range(NST):
            t0 = st_i * (STT // PT)
            for ti in range(STT // PT):
                t = t0 + ti
                p.dma("sync", ht[:], h_scr[t * PT:(t + 1) * PT, :], tag="hld")
                p.op("scalar", lambda e, t=t: e.activation(y2[:], x_sb[:, t * D:(t + 1) * D],
                                                           AF.Identity, bias=nmr_sb[:, t:t + 1],
                                                           scale=r_sb[:, t:t + 1]),
                     r=[x_sb[:, t * D:(t + 1) * D], nmr_sb[:, t:t + 1], r_sb[:, t:t + 1]],
                     w=[y2[:]], tag="y2a")
                p.op("vector", lambda e: e.tensor_add(y2[:], y2[:], ht[:]),
                     r=[y2[:], ht[:]], w=[y2[:]], tag="y2b")
                p.op("scalar", lambda e: e.activation(ev_sb[:], y2[:, 0:512], AF.Square,
                                                      accum_out=stat_sb[:, 8:9]),
                     r=[y2[:, 0:512]], w=[ev_sb[:], stat_sb[:, 8:9]], tag="rms_sq")
                p.op("scalar", lambda e: e.activation(ev_sb[:], y2[:, 512:1024], AF.Square,
                                                      accum_out=stat_sb[:, 13:14]),
                     r=[y2[:, 512:1024], ev_sb[:]], w=[ev_sb[:], stat_sb[:, 13:14]], tag="rms_sq2")
                p.op("vector", lambda e: e.tensor_add(stat_sb[:, 8:9], stat_sb[:, 8:9], stat_sb[:, 13:14]),
                     r=[stat_sb[:, 13:14], stat_sb[:, 8:9]], w=[stat_sb[:, 8:9]], tag="rms_as")
                p.op("vector", lambda e: e.tensor_scalar_mul(stat_sb[:, 9:10], stat_sb[:, 8:9], 1.0 / D),
                     r=[stat_sb[:, 8:9]], w=[stat_sb[:, 9:10]], tag="rms1")
                p.op("scalar", lambda e: e.activation(stat_sb[:, 10:11], stat_sb[:, 9:10],
                                                      AF.Sqrt, bias=EPS6),
                     r=[stat_sb[:, 9:10], EPS6], w=[stat_sb[:, 10:11]], tag="rms2")
                p.op("vector", lambda e: e.reciprocal(stat_sb[:, 11:12], stat_sb[:, 10:11]),
                     r=[stat_sb[:, 10:11]], w=[stat_sb[:, 11:12]], tag="rms3")
                p.op("scalar", lambda e: e.activation(fb[:], y2[:], AF.Identity,
                                                      scale=stat_sb[:, 11:12]),
                     r=[y2[:], stat_sb[:, 11:12]], w=[fb[:]], tag="rms_ap")
                fm_ap = f_fm.rearrange("p (c j) -> p c j", c=NKC)[:, :, ti * PT:(ti + 1) * PT]
                transpose_blocks(fb, fm_ap, ti % 2, NKC, bf=True, tag="ftr")
            for hc in range(32):
                for (scr, buf) in ((w1_scr, w1buf), (w3_scr, w3buf)):
                    srcw = scr[(hc * NKC) * PT:(hc * NKC + NKC) * PT, :].rearrange(
                        "(k p) c -> p k c", p=PT)
                    p.dma("sync", buf.rearrange("p (k c) -> p k c", k=NKC), srcw, tag="w1ld")
                b1 = ps_f[hc % 2]
                b3 = ps_f[2 + hc % 2]
                for kc in range(NKC):
                    p.op("tensor", lambda e, kc=kc, b1=b1: e.matmul(
                            b1[:, 0:STT], w1buf[:, kc * PT:(kc + 1) * PT],
                            f_fm[:, kc * STT:(kc + 1) * STT],
                            start=(kc == 0), stop=(kc == NKC - 1)),
                         r=[w1buf[:, kc * PT:(kc + 1) * PT], f_fm[:, kc * STT:(kc + 1) * STT]],
                         w=[b1[:, 0:STT]], tag="mmw1")
                for kc in range(NKC):
                    p.op("tensor", lambda e, kc=kc, b3=b3: e.matmul(
                            b3[:, 0:STT], w3buf[:, kc * PT:(kc + 1) * PT],
                            f_fm[:, kc * STT:(kc + 1) * STT],
                            start=(kc == 0), stop=(kc == NKC - 1)),
                         r=[w3buf[:, kc * PT:(kc + 1) * PT], f_fm[:, kc * STT:(kc + 1) * STT]],
                         w=[b3[:, 0:STT]], tag="mmw3")
                sbuf = ev_sb[:, 0:STT]
                p.op("scalar", lambda e, sbuf=sbuf, b1=b1: e.activation(sbuf, b1[:, 0:STT], AF.Silu),
                     r=[b1[:, 0:STT]], w=[sbuf], tag="silu")
                p.op("vector", lambda e, hc=hc, sbuf=sbuf, b3=b3: e.tensor_mul(
                        hh[:, hc * STT:(hc + 1) * STT], sbuf, b3[:, 0:STT]),
                     r=[sbuf, b3[:, 0:STT]], w=[hh[:, hc * STT:(hc + 1) * STT]], tag="hmul")
            for half in range(2):
                for hc in range(32):
                    p.dma("sync", w2res[:, hc * 512:(hc + 1) * 512],
                          w2_scr[hc * PT:(hc + 1) * PT, half * 512:(half + 1) * 512], tag="w2ld")
                for ti in range(STT // PT):
                    t = t0 + ti
                    bank = ps_f[4 + ti % 2]
                    for hc in range(32):
                        p.op("tensor", lambda e, hc=hc, ti=ti, bank=bank: e.matmul(
                                bank[:], hh[:, hc * STT + ti * PT: hc * STT + (ti + 1) * PT],
                                w2res[:, hc * 512:(hc + 1) * 512],
                                start=(hc == 0), stop=(hc == 31)),
                             r=[hh[:, hc * STT + ti * PT: hc * STT + (ti + 1) * PT],
                                w2res[:, hc * 512:(hc + 1) * 512]], w=[bank[:]], tag="mmw2")
                    p.dma("gpsimd", ht[:], h_scr[t * PT:(t + 1) * PT, :], tag="hre")
                    p.op("scalar", lambda e, t=t: e.activation(y2[:], x_sb[:, t * D:(t + 1) * D],
                                                               AF.Identity, bias=nmr_sb[:, t:t + 1],
                                                               scale=r_sb[:, t:t + 1]),
                         r=[x_sb[:, t * D:(t + 1) * D], nmr_sb[:, t:t + 1], r_sb[:, t:t + 1]],
                         w=[y2[:]], tag="y2c")
                    p.op("vector", lambda e: e.tensor_add(y2[:], y2[:], ht[:]),
                         r=[y2[:], ht[:]], w=[y2[:]], tag="y2d")
                    xsl = x_sb[:, t * D + half * 512: t * D + (half + 1) * 512]
                    p.op("vector", lambda e, xsl=xsl, bank=bank: e.tensor_add(xsl, xsl, bank[:]),
                         r=[xsl, bank[:]], w=[xsl], tag="resid1")
                    p.op("vector", lambda e, xsl=xsl, half=half: e.tensor_add(
                            xsl, xsl, y2[:, half * 512:(half + 1) * 512]),
                         r=[xsl, y2[:, half * 512:(half + 1) * 512]], w=[xsl], tag="resid2")

    f_state = {}

    def sublayer_l0ffn(app):
        WIN = 512
        NW = S // WIN
        HID2 = 5460
        HID = 2730
        JW = 455
        if app == 0:
            def prep_tabs(inp):
                qs = np.asarray(inp["L0_ffn_q_scale"], np.float64)
                ks = np.asarray(inp["L0_ffn_k_scale"], np.float64)
                hi = np.arange(0, 64, 2, dtype=np.float64)
                inv = 1.0 / 10000.0 ** (hi / 64.0)
                t = np.arange(S, dtype=np.float64)
                fr = np.concatenate([t[:, None] * inv, t[:, None] * inv], 1)
                base = (hi + 25.6) / 89.6
                power = (t - S // 2) / 256.0
                sc = base[None, :] ** power[:, None]
                sc = np.concatenate([sc, sc], 1)
                cos, sin = np.cos(fr), np.sin(fr)
                out = np.zeros((2, S, 256), np.float64)
                for l in range(2):
                    qsl, ksl = qs[l], ks[l]
                    qA = cos * sc * qsl[None, :] * 8.0
                    kA = cos / sc * ksl[None, :]
                    qB = np.zeros((S, 64)); kB = np.zeros((S, 64))
                    qB[:, 0:32] = -sin[:, 0:32] * sc[:, 0:32] * qsl[None, 32:64] * 8.0
                    qB[:, 32:64] = sin[:, 32:64] * sc[:, 32:64] * qsl[None, 0:32] * 8.0
                    kB[:, 0:32] = -sin[:, 0:32] / sc[:, 0:32] * ksl[None, 32:64]
                    kB[:, 32:64] = sin[:, 32:64] / sc[:, 32:64] * ksl[None, 0:32]
                    out[l] = np.concatenate([qA, qB, kA, kB], 1)
                return out.astype(BF)

            def prep_masks(inp):
                m = np.zeros((PT, 2 * PT), np.float32)
                i = np.arange(PT)[:, None]; j = np.arange(PT)[None, :]
                m[:, 0:PT] = np.where(j >= i, 0.0, -1e30)
                m[:, PT:2 * PT] = np.where(j <= i, 0.0, -1e30)
                return m.astype(BF)

            ein("f_wqkv", (2, D, 1536), prep=lambda i: np.ascontiguousarray(i["L0_ffn_wqkv"], np.float32))
            ein("f_wo", (2, 512, D), prep=lambda i: np.ascontiguousarray(i["L0_ffn_wo"], np.float32))
            ein("f_w1", (2, D, HID2), prep=lambda i: np.ascontiguousarray(i["L0_ffn_w1"], np.float32))
            ein("f_b1", (2, HID2), prep=lambda i: np.ascontiguousarray(i["L0_ffn_b1"], np.float32))
            ein("f_b2", (2, D), prep=lambda i: np.ascontiguousarray(i["L0_ffn_b2"], np.float32))
            ein("f_w2", (2, HID, D), prep=lambda i: np.ascontiguousarray(i["L0_ffn_w2"], np.float32))
            ein("f_lnb", (2, PT, 16), prep=lambda i: np.stack(
                [np.concatenate([cols128(i["L0_ffn_ln_g"][l]), cols128(i["L0_ffn_ln_b"][l])], 1)
                 for l in range(2)]))
            ein("f_tabs", (2, S, 256), BF16, prep=prep_tabs)
            ein("f_masks", (PT, 2 * PT), BF16, prep=prep_masks)
            f_state["wqkv"] = nc.dram_tensor("fs_wqkv", [2 * D, 1536], BF16).ap()
            f_state["wo"] = nc.dram_tensor("fs_wo", [2 * 512, D], BF16).ap()
            f_state["w1"] = nc.dram_tensor("fs_w1", [2 * D, HID2], BF16).ap()
            f_state["w2"] = nc.dram_tensor("fs_w2", [2 * HID, D], BF16).ap()
            f_state["rows"] = nc.dram_tensor("fs_rows", [2, 2, HID2], F32).ap()
        scr = f_state
        wqkv_in = nc_ap("f_wqkv"); wo_in = nc_ap("f_wo"); w1_in = nc_ap("f_w1")
        b1_in = nc_ap("f_b1"); b2_in = nc_ap("f_b2"); w2_in = nc_ap("f_w2")
        lnb_in = nc_ap("f_lnb"); tabs_in = nc_ap("f_tabs"); masks_in = nc_ap("f_masks")

        sfx = f"f{app}"
        x_mir = nc.dram_tensor(f"xmir_{sfx}", [T, D], F32).ap()
        st_scr = nc.dram_tensor(f"stsc_{sfx}", [2, 2 * (S + PT), 2], F32).ap()

        if app == 0:
            for l in range(2):
                p.dma("gpsimd", scr["wqkv"][l * D:(l + 1) * D, :], wqkv_in[l], tag="fqkvc")
                p.dma("gpsimd", scr["wo"][l * 512:(l + 1) * 512, :], wo_in[l], tag="fwoc")
                p.dma("gpsimd", scr["w2"][l * HID:(l + 1) * HID, :], w2_in[l], tag="fw2c")
                p.dma("sync", vec_sb[:, 0:16], lnb_in[l], tag="flnb")
                lglb = vec_sb[:, 0:16].rearrange("p (g c) -> p c g", g=2)
                nsl = (HID2 + 511) // 512
                for si in range(nsl):
                    c0 = si * 512
                    cw = min(512, HID2 - c0)
                    stf = arena_b[:, 0:512]
                    stb = B_bf[:, 2048:2048 + 512]
                    rowp = ps_f[5]
                    for kc in range(NKC):
                        p.dma("sync", stf[:, 0:cw], w1_in[l, kc * PT:(kc + 1) * PT, c0:c0 + cw], tag="fw1l")
                        p.op("tensor", lambda e, kc=kc, cw=cw: e.matmul(
                                rowp[0:2, 0:cw], lglb[:, kc], stf[:, 0:cw],
                                start=(kc == 0), stop=(kc == NKC - 1)),
                             r=[vec_sb[:, 0:16], stf[:, 0:cw]], w=[rowp[0:2, 0:cw]], tag="frow")
                        p.op("vector", lambda e, kc=kc, cw=cw: e.tensor_scalar(
                                stb[:, 0:cw], stf[:, 0:cw], vec_sb[:, kc:kc + 1], None, ALU.mult),
                             r=[stf[:, 0:cw], vec_sb[:, kc:kc + 1]], w=[stb[:, 0:cw]], tag="fw1f")
                        p.dma("gpsimd", scr["w1"][l * D + kc * PT:l * D + (kc + 1) * PT, c0:c0 + cw],
                              stb[:, 0:cw], tag="fw1s")
                    rb = arena_b[:, 512:1024]
                    zb = arena_b[:, 1024:1536]
                    p.op("vector", lambda e, cw=cw: e.tensor_copy(rb[0:2, 0:cw], rowp[0:2, 0:cw]),
                         r=[rowp[0:2, 0:cw]], w=[rb[0:2, 0:cw]], tag="frowev")
                    p.op("vector", lambda e, cw=cw: e.memset(zb[0:2, 0:cw], 0.0),
                         w=[zb[0:2, 0:cw]], tag="fb1z")
                    p.dma("sync", zb[1:2, 0:cw], b1_in[l, c0:c0 + cw], tag="fb1l")
                    p.op("vector", lambda e, cw=cw: e.tensor_add(
                            rb[0:2, 0:cw], rb[0:2, 0:cw], zb[0:2, 0:cw]),
                         r=[rb[0:2, 0:cw], zb[0:2, 0:cw]], w=[rb[0:2, 0:cw]], tag="fb1a")
                    p.dma("gpsimd", scr["rows"][l, :, c0:c0 + cw], rb[0:2, 0:cw], tag="frows")

        for t in range(NT):
            p.dma("sync", x_mir[t * PT:(t + 1) * PT, :], x_sb[:, t * D:(t + 1) * D], tag="xmir")
            xt = x_sb[:, t * D:(t + 1) * D]
            p.op("scalar", lambda e, xt=xt: e.activation(sq_f[:], xt[:, 0:512], AF.Square,
                                                         accum_out=stat_sb[:, 6:7]),
                 r=[xt[:, 0:512]], w=[sq_f[:], stat_sb[:, 6:7]], tag="l2sq")
            p.op("scalar", lambda e, xt=xt: e.activation(sq_f[:], xt[:, 512:1024], AF.Square,
                                                         accum_out=stat_sb[:, 7:8]),
                 r=[xt[:, 512:1024], sq_f[:]], w=[sq_f[:], stat_sb[:, 7:8]], tag="l2sq2")
            p.op("vector", lambda e: e.tensor_add(stat_sb[:, 1:2], stat_sb[:, 6:7], stat_sb[:, 7:8]),
                 r=[stat_sb[:, 6:7], stat_sb[:, 7:8]], w=[stat_sb[:, 1:2]], tag="l2ss")
            p.op("vector", lambda e, xt=xt: e.reduce_sum(stat_sb[:, 0:1], xt, axis=AX.X),
                 r=[xt], w=[stat_sb[:, 0:1]], tag="l2sum")
            p.op("vector", lambda e: e.tensor_scalar_mul(stat_sb[:, 2:3], stat_sb[:, 0:1], 1.0 / D),
                 r=[stat_sb[:, 0:1]], w=[stat_sb[:, 2:3]], tag="l2m")
            p.op("vector", lambda e: e.tensor_mul(stat_sb[:, 3:4], stat_sb[:, 2:3], stat_sb[:, 2:3]),
                 r=[stat_sb[:, 2:3]], w=[stat_sb[:, 3:4]], tag="l2m2")
            p.op("vector", lambda e: e.scalar_tensor_tensor(stat_sb[:, 4:5], stat_sb[:, 1:2], 1.0 / D,
                                                            stat_sb[:, 3:4], ALU.mult, ALU.subtract),
                 r=[stat_sb[:, 1:2], stat_sb[:, 3:4]], w=[stat_sb[:, 4:5]], tag="l2v")
            p.op("scalar", lambda e: e.activation(stat_sb[:, 5:6], stat_sb[:, 4:5], AF.Sqrt, bias=EPS5),
                 r=[stat_sb[:, 4:5], EPS5], w=[stat_sb[:, 5:6]], tag="l2sr")
            p.op("vector", lambda e: e.reciprocal(stat_sb[:, 8:9], stat_sb[:, 5:6]),
                 r=[stat_sb[:, 5:6]], w=[stat_sb[:, 8:9]], tag="l2r")
            p.op("vector", lambda e: e.scalar_tensor_tensor(stat_sb[:, 9:10], stat_sb[:, 2:3],
                                                            -1.0, stat_sb[:, 8:9], ALU.mult, ALU.mult),
                 r=[stat_sb[:, 2:3], stat_sb[:, 8:9]], w=[stat_sb[:, 9:10]], tag="l2nm")
            p.op("scalar", lambda e, xt=xt: e.activation(xt, xt, AF.Identity,
                                                         bias=stat_sb[:, 9:10], scale=stat_sb[:, 8:9]),
                 r=[xt, stat_sb[:, 8:9], stat_sb[:, 9:10]], w=[xt], tag="l2ap")

        O_WQKV = 0
        O_X = 12288
        O_FT = O_X
        O_QRAW = O_X + 2080
        O_Q2 = O_QRAW + 512
        O_KRAW = O_Q2 + 512
        O_K2 = O_KRAW + 512
        O_TAB = O_K2 + 512
        O_GRID = O_X
        O_PROBS = O_X + 4096
        O_QT = 17024
        O_KT = 19072
        O_VR = 23168
        O_CTX = 27264
        O_TMP = 29312
        assert O_TMP + 1024 <= 32768

        wqkvR = A_bf[:, O_WQKV:O_WQKV + 8 * 1536].rearrange("p (k c) -> p k c", k=NKC)
        ftA = A_bf[:, O_FT:O_FT + 2080].rearrange("p (b c j) -> p b c j", b=2, c=NKC)
        qraw = A_bf[:, O_QRAW:O_QRAW + 512]
        q2b = A_bf[:, O_Q2:O_Q2 + 512]
        kraw = A_bf[:, O_KRAW:O_KRAW + 512]
        k2b = A_bf[:, O_K2:O_K2 + 512]
        tabb = A_bf[:, O_TAB:O_TAB + 256]
        grid = A_bf[:, O_GRID:O_GRID + 4096].rearrange("p (c q) -> p c q", c=8)
        probs = A_bf[:, O_PROBS:O_PROBS + 640]
        qT = A_bf[:, O_QT:O_QT + 2048].rearrange("p (h c) -> p h c", h=4)
        kT = A_bf[:, O_KT:O_KT + 4096].rearrange("p (h c) -> p h c", h=4)
        vring = A_bf[:, O_VR:O_VR + 4096].rearrange("p (s t c) -> p s t c", s=2, t=4)
        ctxf = A_bf[:, O_CTX:O_CTX + 2048].rearrange("p (i c) -> p i c", i=4)
        rtmp = A_bf[:, O_TMP:O_TMP + 512]
        masks_sb = wsm_sb[:, 0:256]
        p.dma("sync", masks_sb, masks_in, tag="fmaskld")
        wob = B_bf[:, 0:4096].rearrange("p (i c) -> p i c", i=4)

        def part_a(l):
            p.dma("sync", wqkvR, scr["wqkv"][l * D:(l + 1) * D, :].rearrange(
                "(k p) c -> p k c", p=PT), tag="fwqkvR")
            for b in range(BC):
                for w in range(NW):
                    slot = w % 2
                    for ti in range(4):
                        tg = b * 16 + w * 4 + ti
                        fb_i = ti % 2
                        ft = ftA[:, fb_i]
                        xt = x_sb[:, tg * D:(tg + 1) * D]
                        for g in range(2):
                            bank = ps_f[3 + g]
                            for c4 in range(4):
                                cc = g * 4 + c4
                                p.op("tensor", lambda e, cc=cc, c4=c4, bank=bank, xt=xt: e.transpose(
                                        bank[:, c4 * PT:(c4 + 1) * PT], xt[:, cc * PT:(cc + 1) * PT],
                                        identf_sb[:]),
                                     r=[xt[:, cc * PT:(cc + 1) * PT], identf_sb[:]],
                                     w=[bank[:, c4 * PT:(c4 + 1) * PT]], tag="fxtr")
                            dst = ft[:, g * 4:(g + 1) * 4, 1:129]
                            p.op("vector", lambda e, bank=bank, dst=dst: e.tensor_copy(
                                    dst, bank[:].rearrange("p (c j) -> p c j", c=4)),
                                 r=[bank[:]], w=[dst], tag="fxtrev")
                        if w == 0 and ti == 0:
                            p.op("vector", lambda e, ft=ft: e.memset(ft[:, :, 0:1], 0.0),
                                 w=[ft[:, :, 0:1]], tag="fcol0z")
                        else:
                            pft = ftA[:, 1 - fb_i]
                            p.op("vector", lambda e, ft=ft, pft=pft: e.tensor_copy(
                                    ft[:, :, 0:1], pft[:, :, 128:129]),
                                 r=[pft[:, :, 128:129]], w=[ft[:, :, 0:1]], tag="fcol0c")
                        for sl in range(3):
                            bank = ps_f[sl]
                            for kc in range(NKC):
                                lhs = ft[:, kc, 1:129] if kc < 4 else ft[:, kc, 0:128]
                                p.op("tensor", lambda e, lhs=lhs, kc=kc, sl=sl, bank=bank: e.matmul(
                                        bank[:], lhs, wqkvR[:, kc, sl * 512:(sl + 1) * 512],
                                        start=(kc == 0), stop=(kc == NKC - 1)),
                                     r=[lhs, wqkvR[:, kc, sl * 512:(sl + 1) * 512]],
                                     w=[bank[:]], tag="fqkvmm")
                        p.op("vector", lambda e, slot=slot, ti=ti: e.tensor_copy(
                                vring[:, slot, ti, :], ps_f[2][:]),
                             r=[ps_f[2][:]], w=[vring[:, slot, ti, :]], tag="fvev")
                        p.op("vector", lambda e: e.tensor_copy(qraw[:], ps_f[0][:]),
                             r=[ps_f[0][:]], w=[qraw[:]], tag="fqev")
                        p.op("vector", lambda e: e.tensor_copy(kraw[:], ps_f[1][:]),
                             r=[ps_f[1][:]], w=[kraw[:]], tag="fkev")
                        p.dma("sync", tabb, tabs_in[l, w * WIN + ti * PT: w * WIN + (ti + 1) * PT, :],
                              tag="ftab")
                        for (raw, out2, toff) in ((qraw, q2b, 0), (kraw, k2b, 128)):
                            rv = raw.rearrange("p (h d) -> p h d", h=8)
                            ov = out2.rearrange("p (h d) -> p h d", h=8)
                            tv = rtmp.rearrange("p (h d) -> p h d", h=8)
                            p.op("scalar", lambda e, raw=raw: e.activation(
                                    sq_f[:], raw, AF.Square), r=[raw], w=[sq_f[:]], tag="fl2s")
                            p.op("vector", lambda e: e.tensor_reduce(
                                    snorm_sb[:, 0:8], sq_f[:].rearrange("p (h d) -> p h d", h=8),
                                    AX.X, ALU.add),
                                 r=[sq_f[:]], w=[snorm_sb[:, 0:8]], tag="fl2r")
                            p.op("scalar", lambda e: e.activation(snorm_sb[:, 0:8], snorm_sb[:, 0:8],
                                                                  AF.Sqrt),
                                 r=[snorm_sb[:, 0:8]], w=[snorm_sb[:, 0:8]], tag="fl2q")
                            p.op("vector", lambda e: e.tensor_scalar_max(
                                    snorm_sb[:, 0:8], snorm_sb[:, 0:8], 1e-12),
                                 r=[snorm_sb[:, 0:8]], w=[snorm_sb[:, 0:8]], tag="fl2c")
                            p.op("vector", lambda e: e.reciprocal(snorm_sb[:, 8:16], snorm_sb[:, 0:8]),
                                 r=[snorm_sb[:, 0:8]], w=[snorm_sb[:, 8:16]], tag="fl2i")
                            for h in range(8):
                                p.op("vector", lambda e, rv=rv, h=h: e.tensor_scalar(
                                        rv[:, h], rv[:, h], snorm_sb[:, 8 + h:9 + h], None, ALU.mult),
                                     r=[rv[:, h], snorm_sb[:, 8 + h:9 + h]], w=[rv[:, h]], tag="fl2n")
                            tA = tabb[:, toff:toff + 64]
                            tBlo = tabb[:, toff + 64:toff + 96]
                            tBhi = tabb[:, toff + 96:toff + 128]
                            p.op("vector", lambda e, rv=rv, ov=ov, tA=tA: e.tensor_tensor(
                                    ov, rv, tA.rearrange("p (o c) -> p o c", o=1).broadcast_to((PT, 8, 64)),
                                    ALU.mult),
                                 r=[rv[:], tA], w=[ov[:]], tag="frp1")
                            p.op("vector", lambda e, rv=rv, tv=tv, tBlo=tBlo: e.tensor_tensor(
                                    tv[:, :, 0:32], rv[:, :, 32:64],
                                    tBlo.rearrange("p (o c) -> p o c", o=1).broadcast_to((PT, 8, 32)), ALU.mult),
                                 r=[rv[:], tBlo], w=[tv[:, :, 0:32]], tag="frp2")
                            p.op("vector", lambda e, rv=rv, tv=tv, tBhi=tBhi: e.tensor_tensor(
                                    tv[:, :, 32:64], rv[:, :, 0:32],
                                    tBhi.rearrange("p (o c) -> p o c", o=1).broadcast_to((PT, 8, 32)), ALU.mult),
                                 r=[rv[:], tBhi], w=[tv[:, :, 32:64]], tag="frp3")
                            p.op("vector", lambda e, ov=ov, tv=tv: e.tensor_tensor(
                                    ov[:], ov[:], tv[:], ALU.add),
                                 r=[ov[:], tv[:]], w=[ov[:]], tag="frp4")
                        for (src, dstT, rcol) in ((q2b, qT, ti * PT), (k2b, kT, slot * 512 + ti * PT)):
                            bank = ps_bf[0]
                            for hp in range(4):
                                p.op("tensor", lambda e, src=src, hp=hp, bank=bank: e.transpose(
                                        bank[:, hp * PT:(hp + 1) * PT], src[:, hp * PT:(hp + 1) * PT],
                                        ident_sb[:]),
                                     r=[src[:, hp * PT:(hp + 1) * PT], ident_sb[:]],
                                     w=[bank[:, hp * PT:(hp + 1) * PT]], tag="fqkT")
                            dst = dstT[:, :, rcol:rcol + PT]
                            p.op("vector", lambda e, bank=bank, dst=dst: e.tensor_copy(
                                    dst, bank[:, 0:512].rearrange("p (h j) -> p h j", h=4)),
                                 r=[bank[:, 0:512]], w=[dst], tag="fqkTev")
                    for h in range(8):
                        hb = (h % 2) * 64
                        hp = h // 2
                        for qt in range(4):
                            if w == 0:
                                cabs = list(range(0, qt + 1))
                                width = (qt + 1) * PT
                            else:
                                cabs = list(range((w - 1) * 4 + qt, (w - 1) * 4 + qt + 5))
                                width = 640
                            nseg = len(cabs)
                            for ci, ca in enumerate(cabs):
                                jj0 = ci * PT
                                rc = ca % 8
                                (bank, bcol) = (ps_f[3], jj0) if jj0 < 512 else (ps_f[4], jj0 - 512)
                                lhs = qT[hb:hb + 64, hp, qt * PT:(qt + 1) * PT]
                                rhs = kT[hb:hb + 64, hp, rc * PT:(rc + 1) * PT]
                                p.op("tensor", lambda e, lhs=lhs, rhs=rhs, bank=bank, bcol=bcol,
                                               ci=ci: e.matmul(
                                        bank[:, bcol:bcol + PT], lhs, rhs, start=True, stop=True),
                                     r=[lhs, rhs], w=[bank[:, bcol:bcol + PT]], tag="fsc")
                            if w == 0:
                                dcol = qt * PT
                                p.op("vector", lambda e, dcol=dcol: e.tensor_tensor(
                                        ps_f[3][:, dcol:dcol + PT], ps_f[3][:, dcol:dcol + PT],
                                        masks_sb[:, PT:2 * PT], ALU.add),
                                     r=[ps_f[3][:, dcol:dcol + PT], masks_sb[:, PT:2 * PT]],
                                     w=[ps_f[3][:, dcol:dcol + PT]], tag="fmk0")
                            else:
                                p.op("vector", lambda e: e.tensor_tensor(
                                        ps_f[3][:, 0:PT], ps_f[3][:, 0:PT], masks_sb[:, 0:PT], ALU.add),
                                     r=[ps_f[3][:, 0:PT], masks_sb[:, 0:PT]],
                                     w=[ps_f[3][:, 0:PT]], tag="fmkl")
                                p.op("vector", lambda e: e.tensor_tensor(
                                        ps_f[4][:, 0:PT], ps_f[4][:, 0:PT], masks_sb[:, PT:2 * PT],
                                        ALU.add),
                                     r=[ps_f[4][:, 0:PT], masks_sb[:, PT:2 * PT]],
                                     w=[ps_f[4][:, 0:PT]], tag="fmkr")
                            w512 = min(width, 512)
                            p.op("vector", lambda e, w512=w512: e.reduce_max(
                                    stat_sb[:, 0:1], ps_f[3][:, 0:w512], axis=AX.X),
                                 r=[ps_f[3][:, 0:w512]], w=[stat_sb[:, 0:1]], tag="fsm1")
                            if width > 512:
                                p.op("vector", lambda e: e.reduce_max(
                                        stat_sb[:, 1:2], ps_f[4][:, 0:PT], axis=AX.X),
                                     r=[ps_f[4][:, 0:PT]], w=[stat_sb[:, 1:2]], tag="fsm2")
                                p.op("vector", lambda e: e.tensor_max(
                                        stat_sb[:, 0:1], stat_sb[:, 0:1], stat_sb[:, 1:2]),
                                     r=[stat_sb[:, 0:1], stat_sb[:, 1:2]], w=[stat_sb[:, 0:1]],
                                     tag="fsm3")
                            p.op("vector", lambda e: e.tensor_scalar_mul(
                                    stat_sb[:, 2:3], stat_sb[:, 0:1], -1.0),
                                 r=[stat_sb[:, 0:1]], w=[stat_sb[:, 2:3]], tag="fsm4")
                            p.op("scalar", lambda e, w512=w512: e.activation(
                                    probs[:, 0:w512], ps_f[3][:, 0:w512], AF.Exp,
                                    bias=stat_sb[:, 2:3], accum_out=stat_sb[:, 3:4]),
                                 r=[ps_f[3][:, 0:w512], stat_sb[:, 2:3]],
                                 w=[probs[:, 0:w512], stat_sb[:, 3:4]], tag="fex1")
                            if width > 512:
                                p.op("scalar", lambda e: e.activation(
                                        probs[:, 512:640], ps_f[4][:, 0:PT], AF.Exp,
                                        bias=stat_sb[:, 2:3], accum_out=stat_sb[:, 4:5]),
                                     r=[ps_f[4][:, 0:PT], stat_sb[:, 2:3]],
                                     w=[probs[:, 512:640], stat_sb[:, 4:5]], tag="fex2")
                                p.op("vector", lambda e: e.tensor_add(
                                        stat_sb[:, 3:4], stat_sb[:, 3:4], stat_sb[:, 4:5]),
                                     r=[stat_sb[:, 3:4], stat_sb[:, 4:5]], w=[stat_sb[:, 3:4]],
                                     tag="fsma")
                            p.op("vector", lambda e: e.reciprocal(stat_sb[:, 5:6], stat_sb[:, 3:4]),
                                 r=[stat_sb[:, 3:4]], w=[stat_sb[:, 5:6]], tag="fsmr")
                            p.op("vector", lambda e, width=width: e.tensor_scalar(
                                    probs[:, 0:width], probs[:, 0:width], stat_sb[:, 5:6], None,
                                    ALU.mult),
                                 r=[probs[:, 0:width], stat_sb[:, 5:6]], w=[probs[:, 0:width]],
                                 tag="fsmn")
                            for ci, ca in enumerate(cabs):
                                rc = ca % 8
                                bank = ps_bf[1]
                                bcol = (ci % 8) * PT
                                p.op("tensor", lambda e, ci=ci, bcol=bcol, bank=bank: e.transpose(
                                        bank[:, bcol:bcol + PT], probs[:, ci * PT:(ci + 1) * PT],
                                        ident_sb[:]),
                                     r=[probs[:, ci * PT:(ci + 1) * PT], ident_sb[:]],
                                     w=[bank[:, bcol:bcol + PT]], tag="fpT")
                                p.op("vector", lambda e, rc=rc, qt=qt, bcol=bcol, bank=bank:
                                     e.tensor_copy(grid[:, rc, qt * PT:(qt + 1) * PT],
                                                   bank[:, bcol:bcol + PT]),
                                     r=[bank[:, bcol:bcol + PT]],
                                     w=[grid[:, rc, qt * PT:(qt + 1) * PT]], tag="fpTev")
                        for qt in range(4):
                            if w == 0:
                                cas = list(range(0, qt + 1))
                            else:
                                cas = list(range((w - 1) * 4 + qt, (w - 1) * 4 + qt + 5))
                            for ci, ca in enumerate(cas):
                                rc = ca % 8
                                sl2 = (ca // 4) % 2
                                tw = ca % 4
                                lhs = vring[:, sl2, tw, h * 64:(h + 1) * 64]
                                rhs = grid[:, rc, qt * PT:(qt + 1) * PT]
                                p.op("tensor", lambda e, lhs=lhs, rhs=rhs, qt=qt,
                                               st=(ci == 0), sp=(ci == len(cas) - 1): e.matmul(
                                        ps_f[5][0:64, qt * PT:(qt + 1) * PT], lhs, rhs,
                                        start=st, stop=sp),
                                     r=[lhs, rhs], w=[ps_f[5][0:64, qt * PT:(qt + 1) * PT]],
                                     tag="fctx")
                        p.op("vector", lambda e, hb=hb, hp=hp: e.tensor_copy(
                                ctxf[hb:hb + 64, hp, :], ps_f[5][0:64, :]),
                             r=[ps_f[5][0:64, :]], w=[ctxf[hb:hb + 64, hp, :]], tag="fctxev")
                    p.dma("sync", wob, scr["wo"][l * 512:(l + 1) * 512, :].rearrange(
                        "(i p) c -> p i c", p=PT), tag="fwold")
                    for ti in range(4):
                        tg = b * 16 + w * 4 + ti
                        for half in range(2):
                            bank = ps_f[half]
                            for ic in range(4):
                                lhs = ctxf[:, ic, ti * PT:(ti + 1) * PT]
                                rhs = wob[:, ic, half * 512:(half + 1) * 512]
                                p.op("tensor", lambda e, lhs=lhs, rhs=rhs, ic=ic, bank=bank: e.matmul(
                                        bank[:], lhs, rhs, start=(ic == 0), stop=(ic == 3)),
                                     r=[lhs, rhs], w=[bank[:]], tag="fwomm")
                            xsl = x_sb[:, tg * D + half * 512: tg * D + (half + 1) * 512]
                            p.op("vector", lambda e, xsl=xsl, bank=bank: e.tensor_add(
                                    xsl, xsl, bank[:]),
                                 r=[xsl, bank[:]], w=[xsl], tag="fwores")

        B_FT = 0
        B_CSUM = 2080
        B_BIAS = 7540
        B_B2 = 13000
        B_GG = 14024
        B_GGT = 16754
        B_W1S = 28018
        assert B_W1S + 3640 <= 32768
        ftB = A_bf[:, B_FT:B_FT + 2080].rearrange("p (b c j) -> p b c j", b=2, c=NKC)
        csum_r = A_bf[:, B_CSUM:B_CSUM + HID2]
        bias_r = A_bf[:, B_BIAS:B_BIAS + HID2]
        b2_r = A_bf[:, B_B2:B_B2 + D]
        ggb = A_bf[:, B_GG:B_GG + HID]
        ggT = A_bf[:, B_GGT:B_GGT + 11264].rearrange("p (c j) -> p c j", c=22)
        w1s = A_bf[:, B_W1S:B_W1S + 3640].rearrange("p (k c) -> p k c", k=NKC)
        w2s = B_bf[:, 0:2048]
        strow = B_bf[:, 2048:2304]
        b1st = arena_b[:, 1200:1712]

        def part_b(l):
            p.dma("gpsimd", csum_r[0:1, :], scr["rows"][l, 0:1, :], tag="fcsld")
            p.dma("gpsimd", bias_r[0:1, :], scr["rows"][l, 1:2, :], tag="fbild")
            p.dma("sync", b1st[0:1, 0:512], b2_in[l:l + 1, 0:512], tag="fb2l1")
            p.op("vector", lambda e: e.tensor_copy(b2_r[0:1, 0:512], b1st[0:1, 0:512]),
                 r=[b1st[0:1, 0:512]], w=[b2_r[0:1, 0:512]], tag="fb2c1")
            p.dma("sync", b1st[0:1, 0:512], b2_in[l:l + 1, 512:1024], tag="fb2l2")
            p.op("vector", lambda e: e.tensor_copy(b2_r[0:1, 512:1024], b1st[0:1, 0:512]),
                 r=[b1st[0:1, 0:512]], w=[b2_r[0:1, 512:1024]], tag="fb2c2")
            for b in range(BC):
                p.op("vector", lambda e, b=b: e.memset(stat_sb[:, 10:12], 0.0),
                     w=[stat_sb[:, 10:12]], tag="fzp")
                dstz = st_scr[l, b * (S + PT): b * (S + PT) + PT, :].rearrange("(q p) c -> p q c", p=PT)
                p.dma("gpsimd", dstz, stat_sb[:, 10:12].rearrange("p (q c) -> p q c", q=1), tag="fzst")
            for t in range(NT):
                b = t // 16
                xt = x_sb[:, t * D:(t + 1) * D]
                p.op("vector", lambda e, xt=xt, t=t: e.reduce_sum(
                        stats2_sb[:, 0:1], xt[:, 0:512], axis=AX.X),
                     r=[xt[:, 0:512]], w=[stats2_sb[:, 0:1]], tag="fs1")
                p.op("vector", lambda e, xt=xt: e.reduce_sum(
                        stats2_sb[:, 2:3], xt[:, 512:1024], axis=AX.X),
                     r=[xt[:, 512:1024]], w=[stats2_sb[:, 2:3]], tag="fs2")
                p.op("scalar", lambda e, xt=xt: e.activation(sq_f[:], xt[:, 0:512], AF.Square,
                                                             accum_out=stats2_sb[:, 1:2]),
                     r=[xt[:, 0:512]], w=[sq_f[:], stats2_sb[:, 1:2]], tag="fq1")
                p.op("scalar", lambda e, xt=xt: e.activation(sq_f[:], xt[:, 512:1024], AF.Square,
                                                             accum_out=stats2_sb[:, 3:4]),
                     r=[xt[:, 512:1024], sq_f[:]], w=[sq_f[:], stats2_sb[:, 3:4]], tag="fq2")
                dst = st_scr[l, b * (S + PT) + (t % 16) * PT + 1:
                             b * (S + PT) + (t % 16) * PT + 1 + PT, :].rearrange(
                    "(q p) c -> p q c", p=PT)
                p.dma("gpsimd", dst, stats2_sb[:, 2:4].rearrange("p (q c) -> p q c", q=1), tag="fsspl")
                p.op("vector", lambda e, t=t: e.tensor_copy(s1q1_sb[:, 2 * t:2 * t + 2], stats2_sb[:, 0:2]),
                     r=[stats2_sb[:, 0:2]], w=[s1q1_sb[:, 2 * t:2 * t + 2]], tag="fk1")
            for st_i in range(8):
                for ti4 in range(4):
                    t = st_i * 4 + ti4
                    b = t // 16
                    ft = ftB[:, ti4 % 2]
                    xt = x_sb[:, t * D:(t + 1) * D]
                    for g in range(2):
                        bank = ps_f[3 + g]
                        for c4 in range(4):
                            cc = g * 4 + c4
                            p.op("tensor", lambda e, cc=cc, c4=c4, bank=bank, xt=xt: e.transpose(
                                    bank[:, c4 * PT:(c4 + 1) * PT], xt[:, cc * PT:(cc + 1) * PT],
                                    identf_sb[:]),
                                 r=[xt[:, cc * PT:(cc + 1) * PT], identf_sb[:]],
                                 w=[bank[:, c4 * PT:(c4 + 1) * PT]], tag="fbtr")
                        dst = ft[:, g * 4:(g + 1) * 4, 1:129]
                        p.op("vector", lambda e, bank=bank, dst=dst: e.tensor_copy(
                                dst, bank[:].rearrange("p (c j) -> p c j", c=4)),
                             r=[bank[:]], w=[dst], tag="fbtrev")
                    if t % 16 == 0:
                        p.op("vector", lambda e, ft=ft: e.memset(ft[:, :, 0:1], 0.0),
                             w=[ft[:, :, 0:1]], tag="fbc0z")
                    else:
                        pft = ftB[:, 1 - ti4 % 2]
                        if ti4 == 0:
                            p.op("vector", lambda e, ft=ft: e.tensor_copy(
                                    ft[:, :, 0:1], prevcol_sb[:].rearrange("p (c j) -> p c j", j=1)),
                                 r=[prevcol_sb[:]], w=[ft[:, :, 0:1]], tag="fbc0p")
                        else:
                            p.op("vector", lambda e, ft=ft, pft=pft: e.tensor_copy(
                                    ft[:, :, 0:1], pft[:, :, 128:129]),
                                 r=[pft[:, :, 128:129]], w=[ft[:, :, 0:1]], tag="fbc0c")
                    if ti4 == 3:
                        p.op("vector", lambda e, ft=ft: e.tensor_copy(
                                prevcol_sb[:].rearrange("p (c j) -> p c j", j=1), ft[:, :, 128:129]),
                             r=[ft[:, :, 128:129]], w=[prevcol_sb[:]], tag="fbsave")
                    src = st_scr[l, b * (S + PT) + (t % 16) * PT: b * (S + PT) + (t % 16) * PT + PT,
                                 :].rearrange("(q p) c -> p q c", p=PT)
                    p.dma("sync", stats2_sb[:, 4:6].rearrange("p (q c) -> p q c", q=1), src, tag="fsld")
                    p.op("vector", lambda e, t=t: e.tensor_add(stats2_sb[:, 6:7],
                                                               s1q1_sb[:, 2 * t:2 * t + 1],
                                                               stats2_sb[:, 4:5]),
                         r=[s1q1_sb[:, 2 * t:2 * t + 1], stats2_sb[:, 4:5]],
                         w=[stats2_sb[:, 6:7]], tag="fm1")
                    p.op("vector", lambda e, t=t: e.tensor_add(stats2_sb[:, 7:8],
                                                               s1q1_sb[:, 2 * t + 1:2 * t + 2],
                                                               stats2_sb[:, 5:6]),
                         r=[s1q1_sb[:, 2 * t + 1:2 * t + 2], stats2_sb[:, 5:6]],
                         w=[stats2_sb[:, 7:8]], tag="fm2")
                    p.op("vector", lambda e: e.tensor_scalar_mul(stats2_sb[:, 6:7],
                                                                 stats2_sb[:, 6:7], 1.0 / D),
                         r=[stats2_sb[:, 6:7]], w=[stats2_sb[:, 6:7]], tag="fm3")
                    p.op("vector", lambda e: e.tensor_mul(stats2_sb[:, 8:9], stats2_sb[:, 6:7],
                                                          stats2_sb[:, 6:7]),
                         r=[stats2_sb[:, 6:7]], w=[stats2_sb[:, 8:9]], tag="fm4")
                    p.op("vector", lambda e: e.scalar_tensor_tensor(
                            stats2_sb[:, 9:10], stats2_sb[:, 7:8], 1.0 / D, stats2_sb[:, 8:9],
                            ALU.mult, ALU.subtract),
                         r=[stats2_sb[:, 7:8], stats2_sb[:, 8:9]], w=[stats2_sb[:, 9:10]], tag="fm5")
                    p.op("scalar", lambda e: e.activation(stats2_sb[:, 10:11], stats2_sb[:, 9:10],
                                                          AF.Sqrt, bias=EPS5),
                         r=[stats2_sb[:, 9:10], EPS5], w=[stats2_sb[:, 10:11]], tag="fm6")
                    p.op("vector", lambda e: e.reciprocal(stats2_sb[:, 11:12], stats2_sb[:, 10:11]),
                         r=[stats2_sb[:, 10:11]], w=[stats2_sb[:, 11:12]], tag="fm7")
                    p.op("vector", lambda e: e.tensor_scalar_mul(stats2_sb[:, 12:13],
                                                                 stats2_sb[:, 6:7], -1.0),
                         r=[stats2_sb[:, 6:7]], w=[stats2_sb[:, 12:13]], tag="fm8")
                    for (col, roff) in ((12, 0), (10, 128)):
                        p.op("tensor", lambda e, col=col: e.transpose(
                                ps_f[5][0:1, 0:PT],
                                stats2_sb[:, col:col + 1], identf_sb[:]),
                             r=[stats2_sb[:, col:col + 1], identf_sb[:]],
                             w=[ps_f[5][0:1, 0:PT]], tag="fstT")
                        p.op("vector", lambda e, roff=roff: e.tensor_copy(
                                strow[0:1, roff:roff + PT], ps_f[5][0:1, 0:PT]),
                             r=[ps_f[5][0:1, 0:PT]], w=[strow[0:1, roff:roff + PT]], tag="fstTe")
                    for j in range(6):
                        for (jj, bank) in ((j, ps_f[0]), (j + 6, ps_f[1])):
                            c0 = jj * JW
                            p.dma("sync", w1s[:, :, 0:JW],
                                  scr["w1"][l * D:(l + 1) * D, c0:c0 + JW].rearrange(
                                      "(k p) c -> p k c", p=PT), tag="fw1st")
                            for kc in range(NKC):
                                lhs = ft[:, kc, 1:129] if kc < 4 else ft[:, kc, 0:128]
                                p.op("tensor", lambda e, lhs=lhs, kc=kc, bank=bank: e.matmul(
                                        bank[:, 0:JW], lhs, w1s[:, kc, 0:JW],
                                        start=(kc == 0), stop=False),
                                     r=[lhs, w1s[:, kc, 0:JW]], w=[bank[:, 0:JW]], tag="fh1mm")
                            p.op("tensor", lambda e, c0=c0, bank=bank: e.matmul(
                                    bank[:, 0:JW], strow[0:1, 0:PT], csum_r[0:1, c0:c0 + JW],
                                    start=False, stop=False),
                                 r=[strow[0:1, 0:PT], csum_r[0:1, c0:c0 + JW]],
                                 w=[bank[:, 0:JW]], tag="fk1a")
                            p.op("tensor", lambda e, c0=c0, bank=bank: e.matmul(
                                    bank[:, 0:JW], strow[0:1, 128:256], bias_r[0:1, c0:c0 + JW],
                                    start=False, stop=True),
                                 r=[strow[0:1, 128:256], bias_r[0:1, c0:c0 + JW]],
                                 w=[bank[:, 0:JW]], tag="fk1b")
                        p.op("scalar", lambda e: e.activation(ev_sb[:, 0:JW], ps_f[1][:, 0:JW],
                                                              AF.Gelu, scale=stats2_sb[:, 11:12]),
                             r=[ps_f[1][:, 0:JW], stats2_sb[:, 11:12]], w=[ev_sb[:, 0:JW]], tag="fgel")
                        p.op("vector", lambda e, j=j: e.scalar_tensor_tensor(
                                ggb[:, j * JW:(j + 1) * JW], ps_f[0][:, 0:JW], stats2_sb[:, 11:12],
                                ev_sb[:, 0:JW], ALU.mult, ALU.mult),
                             r=[ps_f[0][:, 0:JW], stats2_sb[:, 11:12], ev_sb[:, 0:JW]],
                             w=[ggb[:, j * JW:(j + 1) * JW]], tag="fgg")
                    for g in range(6):
                        bank = ps_bf[g % 2]
                        nch = 4 if g < 5 else 2
                        for c4 in range(nch):
                            cc = g * 4 + c4
                            cw = PT if cc < 21 else HID - 21 * PT
                            p.op("tensor", lambda e, cc=cc, c4=c4, cw=cw, bank=bank: e.transpose(
                                    bank[0:cw, c4 * PT:(c4 + 1) * PT],
                                    ggb[:, cc * PT:cc * PT + cw], ident_sb[:]),
                                 r=[ggb[:, cc * PT:cc * PT + cw], ident_sb[:]],
                                 w=[bank[:, c4 * PT:(c4 + 1) * PT]], tag="fggT")
                        dst = ggT[:, g * 4:g * 4 + nch, ti4 * PT:(ti4 + 1) * PT]
                        p.op("vector", lambda e, bank=bank, dst=dst, nch=nch: e.tensor_copy(
                                dst, bank[:, 0:nch * PT].rearrange("p (c j) -> p c j", c=nch)),
                             r=[bank[:, 0:nch * PT]], w=[dst], tag="fggTev")
                for half in range(2):
                    for hc in range(22):
                        cw = PT if hc < 21 else HID - 21 * PT
                        p.dma("sync", w2s[0:cw, 0:512],
                              scr["w2"][l * HID + hc * PT: l * HID + hc * PT + cw,
                                        half * 512:(half + 1) * 512], tag="fw2st")
                        for ti4 in range(4):
                            t = st_i * 4 + ti4
                            bank = ps_f[2 + ti4]
                            p.op("tensor", lambda e, hc=hc, cw=cw, ti4=ti4, bank=bank: e.matmul(
                                    bank[:], ggT[0:cw, hc, ti4 * PT:(ti4 + 1) * PT], w2s[0:cw, 0:512],
                                    start=(hc == 0), stop=False),
                                 r=[ggT[0:cw, hc, ti4 * PT:(ti4 + 1) * PT], w2s[0:cw, 0:512]],
                                 w=[bank[:]], tag="fw2mm")
                    for ti4 in range(4):
                        t = st_i * 4 + ti4
                        bank = ps_f[2 + ti4]
                        p.op("tensor", lambda e, half=half, bank=bank: e.matmul(
                                bank[:], ones_bf[0:1, 0:PT], b2_r[0:1, half * 512:(half + 1) * 512],
                                start=False, stop=True),
                             r=[ones_bf[0:1, 0:PT], b2_r[0:1, half * 512:(half + 1) * 512]],
                             w=[bank[:]], tag="fb2mm")
                        xsl = x_sb[:, t * D + half * 512: t * D + (half + 1) * 512]
                        p.op("vector", lambda e, xsl=xsl, bank=bank: e.tensor_add(xsl, xsl, bank[:]),
                             r=[xsl, bank[:]], w=[xsl], tag="fb2res")

        def dbg_dump(name):
            if not DBG:
                return
            dt_ = nc.dram_tensor(name, [T, D], F32, kind="ExternalOutput").ap()
            for t in range(NT):
                p.dma("gpsimd", dt_[t * PT:(t + 1) * PT, :], x_sb[:, t * D:(t + 1) * D],
                      tag="dbgd")

        dbg_dump(f"dbg_{sfx}_ln")
        for l in range(2):
            part_a(l)
            dbg_dump(f"dbg_{sfx}_a{l}")
            part_b(l)
            dbg_dump(f"dbg_{sfx}_b{l}")

        for t in range(NT):
            p.dma("sync", arena_b[:, 0:1024], x_mir[t * PT:(t + 1) * PT, :], tag="fxmr")
            xt = x_sb[:, t * D:(t + 1) * D]
            p.op("vector", lambda e, xt=xt: e.tensor_add(xt, xt, arena_b[:, 0:1024]),
                 r=[xt, arena_b[:, 0:1024]], w=[xt], tag="ffin")

    def sublayer3():
        NH = 16
        HD = 64
        ein("b_wq", (D, D), prep=lambda i: np.ascontiguousarray(i["L0_attn_b_wq"].T, np.float32))
        ein("b_wk", (D, D), prep=lambda i: np.ascontiguousarray(i["L0_attn_b_wk"].T, np.float32))
        ein("b_wv", (D, D), prep=lambda i: np.ascontiguousarray(i["L0_attn_b_wv"].T, np.float32))
        ein("b_wo", (D, D), prep=lambda i: np.ascontiguousarray(i["L0_attn_b_wo"].T, np.float32))
        ein("b_a4", (4, 16, D), prep=lambda i: np.ascontiguousarray(np.stack(
            [i["L0_attn_b_aq"], i["L0_attn_b_ak"], i["L0_attn_b_av"],
             i["L0_attn_b_ao"]]), np.float32))
        ein("b_b4", (4, 16, D), prep=lambda i: np.ascontiguousarray(np.stack(
            [i["L0_attn_b_bq"].T, i["L0_attn_b_bk"].T, i["L0_attn_b_bv"].T,
             i["L0_attn_b_bo"].T]), np.float32))
        wq_in = nc_ap("b_wq"); wk_in = nc_ap("b_wk"); wv_in = nc_ap("b_wv"); wo_in = nc_ap("b_wo")
        a4_in = nc_ap("b_a4"); b4_in = nc_ap("b_b4")
        w_scr = nc.dram_tensor("b_wscr", [4 * D, D], BF16).ap()
        q_scr = nc.dram_tensor("b_qscr", [D, T], BF16).ap()
        k_scr = nc.dram_tensor("b_kscr", [D, T], BF16).ap()
        v_scr = nc.dram_tensor("b_vscr", [D, T], BF16).ap()

        stage = arena_b[:, 0:1024]
        stgb = B_bf[:, 2048:2048 + 1024]
        a_sb = B_bf[0:16, 3072:4096]
        b_sb = wsm_sb[0:16, 0:1024]
        for wi, (srcw, scale) in enumerate(((wq_in, 0.125), (wk_in, 1.0), (wv_in, 1.0),
                                            (wo_in, 1.0))):
            p.dma("gpsimd", a_sb, a4_in[wi], tag="bal0")
            p.dma("sync", stage[0:16, 0:D], b4_in[wi], tag="bbl0")
            p.op("vector", lambda e: e.tensor_scalar(b_sb, stage[0:16, 0:D], 0.0625, None, ALU.mult),
                 r=[stage[0:16, 0:D]], w=[b_sb], tag="bbf")
            for kc in range(NKC):
                p.dma("sync", stage[:, 0:D], srcw[kc * PT:(kc + 1) * PT, :], tag="bwl")
                for half in range(2):
                    bank = ps_f[4 + half]
                    p.op("tensor", lambda e, kc=kc, half=half, bank=bank: e.matmul(
                            bank[:], a_sb[:, kc * PT:(kc + 1) * PT],
                            b_sb[:, half * 512:(half + 1) * 512], start=True, stop=True),
                         r=[a_sb[:, kc * PT:(kc + 1) * PT], b_sb[:, half * 512:(half + 1) * 512]],
                         w=[bank[:]], tag="bcorr")
                    p.op("vector", lambda e, half=half, bank=bank: e.tensor_add(
                            stage[:, half * 512:(half + 1) * 512],
                            stage[:, half * 512:(half + 1) * 512], bank[:]),
                         r=[stage[:, half * 512:(half + 1) * 512], bank[:]],
                         w=[stage[:, half * 512:(half + 1) * 512]], tag="bcadd")
                p.op("vector", lambda e, scale=scale: e.tensor_scalar(
                        stgb[:, 0:D], stage[:, 0:D], float(scale), None, ALU.mult),
                     r=[stage[:, 0:D]], w=[stgb[:, 0:D]], tag="bwc")
                p.dma("gpsimd", w_scr[wi * D + kc * PT: wi * D + (kc + 1) * PT, :],
                      stgb[:, 0:D], tag="bws")

        for t in range(NT):
            ln_tile(t, XT_bf[:])
            fm_ap = A_bf[:].rearrange("p (c j) -> p c j", c=NKC)[:, :, t * PT:(t + 1) * PT]
            transpose_blocks(XT_bf[:], fm_ap, t % 2, NKC, bf=True, tag=f"s3tr{t}")
        xn_fm = A_bf

        wst = XT_bf[:].rearrange("p (k c) -> p k c", k=NKC)
        est = ev_sb.bitcast(BF16)
        for wi, scr_out in enumerate((q_scr, k_scr, v_scr)):
            for oc in range(NKC):
                p.dma("sync", wst, w_scr[wi * D:(wi + 1) * D, oc * PT:(oc + 1) * PT].rearrange(
                    "(k p) c -> p k c", p=PT), tag="bwst")
                for ts in range(8):
                    bank = ps_f[ts % 2]
                    for kc in range(NKC):
                        rhs = xn_fm[:, kc * T + ts * 512:kc * T + (ts + 1) * 512]
                        p.op("tensor", lambda e, kc=kc, rhs=rhs, bank=bank: e.matmul(
                                bank[:], wst[:, kc], rhs, start=(kc == 0), stop=(kc == NKC - 1)),
                             r=[wst[:, kc], rhs], w=[bank[:]], tag="bqmm")
                    sl2 = est[:, (ts % 2) * 512:(ts % 2) * 512 + 512]
                    p.op("vector", lambda e, bank=bank, sl2=sl2: e.tensor_copy(sl2, bank[:]),
                         r=[bank[:]], w=[sl2], tag="bqev")
                    p.dma("gpsimd", scr_out[oc * PT:(oc + 1) * PT, ts * 512:(ts + 1) * 512],
                          sl2, tag="bqst")

        O3_CTX = 0
        O3_GRID = 16384
        O3_QK = 24576
        O3_V = 28672
        O3_VF = 29696
        O3_PROBS = 31744
        ctx3 = A_bf[:, O3_CTX:O3_CTX + 16384].rearrange("p (c j) -> p c j", c=8)
        grid3 = A_bf[:, O3_GRID:O3_GRID + 8192].rearrange("p (c j) -> p c j", c=16)
        wo3_res = A_bf[:, O3_GRID:O3_GRID + 8192].rearrange("p (k c) -> p k c", k=NKC)
        qk3 = A_bf[:, O3_QK:O3_QK + 4096]
        v3 = A_bf[:, O3_V:O3_V + 1024].rearrange("p (t c) -> p t c", t=16)
        vf3 = A_bf[:, O3_VF:O3_VF + 2048]
        pr3 = A_bf[:, O3_PROBS:O3_PROBS + 512]
        sc3 = arena_b
        masks3_sb = wsm_sb[:, 128:256]
        p.dma("sync", masks3_sb, nc_ap("f_masks")[:, 128:256], tag="bmask")
        for b in range(BC):
            for h in range(NH):
                p.dma("sync", qk3[0:64, 0:S], q_scr[h * HD:(h + 1) * HD, b * S:(b + 1) * S], tag="bql")
                p.dma("sync", qk3[0:64, S:2 * S], k_scr[h * HD:(h + 1) * HD, b * S:(b + 1) * S], tag="bkl")
                p.dma("sync", vf3[0:64, :], v_scr[h * HD:(h + 1) * HD, b * S:(b + 1) * S], tag="bvl")
                for tt in range(16):
                    bank = ps_bf[0]
                    p.op("tensor", lambda e, tt=tt, bank=bank: e.transpose(
                            bank[:, 0:64], vf3[0:64, tt * PT:(tt + 1) * PT], ident_sb[0:64, 0:64]),
                         r=[vf3[0:64, tt * PT:(tt + 1) * PT], ident_sb[0:64, 0:64]],
                         w=[bank[:, 0:64]], tag="bvT")
                    p.op("vector", lambda e, tt=tt, bank=bank: e.tensor_copy(
                            v3[:, tt, :], bank[:, 0:64]),
                         r=[bank[:, 0:64]], w=[v3[:, tt, :]], tag="bvTe")
                for qs in range(4):
                    for lqt in range(4):
                        qt = qs * 4 + lqt
                        nb = qt // 4 + 1
                        for bi in range(nb):
                            bw = 512 if bi < qt // 4 else (qt % 4 + 1) * PT
                            bank = ps_f[bi % 2]
                            lhs = qk3[0:64, qt * PT:(qt + 1) * PT]
                            rhs = qk3[0:64, S + bi * 512:S + bi * 512 + bw]
                            p.op("tensor", lambda e, lhs=lhs, rhs=rhs, bw=bw, bank=bank: e.matmul(
                                    bank[:, 0:bw], lhs, rhs, start=True, stop=True),
                                 r=[lhs, rhs], w=[bank[:, 0:bw]], tag="bsc")
                            if bi == nb - 1:
                                dcol = bw - PT
                                p.op("vector", lambda e, bank=bank, dcol=dcol: e.tensor_tensor(
                                        bank[:, dcol:dcol + PT], bank[:, dcol:dcol + PT],
                                        masks3_sb, ALU.add),
                                     r=[bank[:, dcol:dcol + PT], masks3_sb],
                                     w=[bank[:, dcol:dcol + PT]], tag="bmk")
                            p.op("vector", lambda e, bank=bank, bi=bi, bw=bw: e.tensor_copy(
                                    sc3[:, bi * 512:bi * 512 + bw], bank[:, 0:bw]),
                                 r=[bank[:, 0:bw]], w=[sc3[:, bi * 512:bi * 512 + bw]], tag="bscv")
                        span = (qt + 1) * PT
                        p.op("vector", lambda e, span=span: e.reduce_max(
                                stat_sb[:, 0:1], sc3[:, 0:span], axis=AX.X),
                             r=[sc3[:, 0:span]], w=[stat_sb[:, 0:1]], tag="bmx")
                        p.op("vector", lambda e: e.tensor_scalar_mul(stat_sb[:, 1:2],
                                                                     stat_sb[:, 0:1], -1.0),
                             r=[stat_sb[:, 0:1]], w=[stat_sb[:, 1:2]], tag="bnm")
                        for bi in range(nb):
                            bw = 512 if bi < qt // 4 else (qt % 4 + 1) * PT
                            p.op("scalar", lambda e, bi=bi, bw=bw: e.activation(
                                    sc3[:, bi * 512:bi * 512 + bw], sc3[:, bi * 512:bi * 512 + bw],
                                    AF.Exp, bias=stat_sb[:, 1:2],
                                    accum_out=stat_sb[:, 2 + bi:3 + bi]),
                                 r=[sc3[:, bi * 512:bi * 512 + bw], stat_sb[:, 1:2]],
                                 w=[sc3[:, bi * 512:bi * 512 + bw], stat_sb[:, 2 + bi:3 + bi]],
                                 tag="bex")
                        for bi in range(1, nb):
                            p.op("vector", lambda e, bi=bi: e.tensor_add(
                                    stat_sb[:, 2:3], stat_sb[:, 2:3], stat_sb[:, 2 + bi:3 + bi]),
                                 r=[stat_sb[:, 2:3], stat_sb[:, 2 + bi:3 + bi]],
                                 w=[stat_sb[:, 2:3]], tag="bsa")
                        p.op("vector", lambda e: e.reciprocal(stat_sb[:, 6:7], stat_sb[:, 2:3]),
                             r=[stat_sb[:, 2:3]], w=[stat_sb[:, 6:7]], tag="brc")
                        for kc in range(qt + 1):
                            p.op("vector", lambda e, kc=kc: e.tensor_scalar(
                                    pr3[:, 0:PT], sc3[:, kc * PT:(kc + 1) * PT],
                                    stat_sb[:, 6:7], None, ALU.mult),
                                 r=[sc3[:, kc * PT:(kc + 1) * PT], stat_sb[:, 6:7]],
                                 w=[pr3[:, 0:PT]], tag="bnr")
                            bank = ps_bf[kc % 2]
                            p.op("tensor", lambda e, bank=bank: e.transpose(
                                    bank[:, 0:PT], pr3[:, 0:PT], ident_sb[:]),
                                 r=[pr3[:, 0:PT], ident_sb[:]], w=[bank[:, 0:PT]], tag="bpT")
                            p.op("vector", lambda e, kc=kc, lqt=lqt, bank=bank: e.tensor_copy(
                                    grid3[:, kc, lqt * PT:(lqt + 1) * PT], bank[:, 0:PT]),
                                 r=[bank[:, 0:PT]], w=[grid3[:, kc, lqt * PT:(lqt + 1) * PT]],
                                 tag="bpTe")
                    nkc3 = qs * 4 + 4
                    for kc in range(nkc3):
                        qlo = max(0, kc - qs * 4)
                        lhs = v3[:, kc, :]
                        rhs = grid3[:, kc, qlo * PT:512]
                        p.op("tensor", lambda e, lhs=lhs, rhs=rhs, qlo=qlo,
                                       st=(kc == 0), sp=(kc == nkc3 - 1): e.matmul(
                                ps_f[2][0:64, qlo * PT:512], lhs, rhs, start=st, stop=sp),
                             r=[lhs, rhs], w=[ps_f[2][0:64, qlo * PT:512]], tag="bctx")
                    hb = (h % 2) * 64
                    p.op("vector", lambda e, hb=hb, h=h, qs=qs: e.tensor_copy(
                            ctx3[hb:hb + 64, h // 2, qs * 512:(qs + 1) * 512], ps_f[2][0:64, :]),
                         r=[ps_f[2][0:64, :]], w=[ctx3[hb:hb + 64, h // 2, qs * 512:(qs + 1) * 512]],
                         tag="bctxe")
            p.dma("sync", wo3_res, w_scr[3 * D:4 * D, :].rearrange("(k p) c -> p k c", p=PT),
                  tag="bwo3")
            for tt in range(16):
                tg = b * 16 + tt
                for half in range(2):
                    bank = ps_f[3 + half]
                    for kc in range(NKC):
                        lhs = ctx3[:, kc, tt * PT:(tt + 1) * PT]
                        p.op("tensor", lambda e, lhs=lhs, kc=kc, half=half, bank=bank: e.matmul(
                                bank[:], lhs, wo3_res[:, kc, half * 512:(half + 1) * 512],
                                start=(kc == 0), stop=(kc == 7)),
                             r=[lhs, wo3_res[:, kc, half * 512:(half + 1) * 512]],
                             w=[bank[:]], tag="bo2mm")
                    xsl = x_sb[:, tg * D + half * 512: tg * D + (half + 1) * 512]
                    p.op("vector", lambda e, xsl=xsl, bank=bank: e.tensor_add(xsl, xsl, bank[:]),
                         r=[xsl, bank[:]], w=[xsl], tag="bores")

        if DBG:
            dt_ = nc.dram_tensor("dbg_s3", [T, D], F32, kind="ExternalOutput").ap()
            for t in range(NT):
                p.dma("gpsimd", dt_[t * PT:(t + 1) * PT, :], x_sb[:, t * D:(t + 1) * D], tag="dbg3")

    if N_SUB >= 1:
        sublayer1()
        if DBG:
            dt1_ = nc.dram_tensor("dbg_s1", [T, D], F32, kind="ExternalOutput").ap()
            for t in range(NT):
                p.dma("gpsimd", dt1_[t * PT:(t + 1) * PT, :], x_sb[:, t * D:(t + 1) * D],
                      tag="dbg1")
    if N_SUB >= 2:
        sublayer_l0ffn(0)
    if N_SUB >= 3:
        sublayer3()
    if N_SUB >= 4:
        sublayer_l0ffn(1)

    def sublayer5():
        NH = 16
        HD = 64
        ein("c_wq", (D, D), prep=lambda i: np.ascontiguousarray(
            i["L1_attn_a_query_proj"].reshape(D, D).T, np.float32))
        ein("c_wkv", (D, 2 * HD), prep=lambda i: np.ascontiguousarray(
            np.concatenate([i["L1_attn_a_key_proj"], i["L1_attn_a_value_proj"]], 1), np.float32))
        ein("c_wo", (D, D), prep=lambda i: np.ascontiguousarray(
            i["L1_attn_a_output_proj"].transpose(1, 2, 0).reshape(D, D), np.float32))
        wq_in = nc_ap("c_wq"); wkv_in = nc_ap("c_wkv"); wo_in = nc_ap("c_wo")
        w_scr = nc.dram_tensor("c_wscr", [2 * D, D], BF16).ap()
        kv_scr = nc.dram_tensor("c_kvscr", [D, 2 * HD], BF16).ap()
        q_scr = nc.dram_tensor("c_qscr", [D, T], BF16).ap()
        kv_fm_scr = nc.dram_tensor("c_kvfm", [2 * HD, T], BF16).ap()

        stage = arena_b[:, 0:1024]
        stgb = B_bf[:, 2048:3072]
        for wi, (srcw, scale) in enumerate(((wq_in, 0.125), (wo_in, 1.0))):
            for kc in range(NKC):
                p.dma("sync", stage[:, 0:D], srcw[kc * PT:(kc + 1) * PT, :], tag="cwl")
                p.op("vector", lambda e, scale=scale: e.tensor_scalar(
                        stgb[:, 0:D], stage[:, 0:D], float(scale), None, ALU.mult),
                     r=[stage[:, 0:D]], w=[stgb[:, 0:D]], tag="cwc")
                p.dma("gpsimd", w_scr[wi * D + kc * PT: wi * D + (kc + 1) * PT, :],
                      stgb[:, 0:D], tag="cws")
        p.dma("gpsimd", kv_scr[:, :], wkv_in, tag="ckvc")

        for t in range(NT):
            ln_tile(t, XT_bf[:])
            fm_ap = A_bf[:].rearrange("p (c j) -> p c j", c=NKC)[:, :, t * PT:(t + 1) * PT]
            transpose_blocks(XT_bf[:], fm_ap, t % 2, NKC, bf=True, tag=f"s5tr{t}")
        xn_fm = A_bf

        wst = XT_bf[:].rearrange("p (k c) -> p k c", k=NKC)
        est = ev_sb.bitcast(BF16)
        for oc in range(NKC):
            p.dma("sync", wst, w_scr[0:D, oc * PT:(oc + 1) * PT].rearrange(
                "(k p) c -> p k c", p=PT), tag="cwst")
            for ts in range(8):
                bank = ps_f[ts % 2]
                for kc in range(NKC):
                    rhs = xn_fm[:, kc * T + ts * 512:kc * T + (ts + 1) * 512]
                    p.op("tensor", lambda e, kc=kc, rhs=rhs, bank=bank: e.matmul(
                            bank[:], wst[:, kc], rhs, start=(kc == 0), stop=(kc == NKC - 1)),
                         r=[wst[:, kc], rhs], w=[bank[:]], tag="cqmm")
                sl2 = est[:, (ts % 2) * 512:(ts % 2) * 512 + 512]
                p.op("vector", lambda e, bank=bank, sl2=sl2: e.tensor_copy(sl2, bank[:]),
                     r=[bank[:]], w=[sl2], tag="cqev")
                p.dma("gpsimd", q_scr[oc * PT:(oc + 1) * PT, ts * 512:(ts + 1) * 512],
                      sl2, tag="cqst")
        p.dma("sync", wst[:, :, 0:PT], kv_scr[:, :].rearrange("(k p) c -> p k c", p=PT), tag="ckvl")
        for ts in range(8):
            bank = ps_f[ts % 2]
            for kc in range(NKC):
                rhs = xn_fm[:, kc * T + ts * 512:kc * T + (ts + 1) * 512]
                p.op("tensor", lambda e, kc=kc, rhs=rhs, bank=bank: e.matmul(
                        bank[:], wst[:, kc, 0:PT], rhs, start=(kc == 0), stop=(kc == NKC - 1)),
                     r=[wst[:, kc, 0:PT], rhs], w=[bank[:]], tag="ckvmm")
            sl2 = est[:, (ts % 2) * 512:(ts % 2) * 512 + 512]
            p.op("vector", lambda e, bank=bank, sl2=sl2: e.tensor_copy(sl2, bank[:]),
                 r=[bank[:]], w=[sl2], tag="ckvev")
            p.dma("gpsimd", kv_fm_scr[:, ts * 512:(ts + 1) * 512], sl2, tag="ckvst")

        O5_CTX = 0
        O5_GRID = 16384
        O5_Q = 24576
        O5_K = 26624
        O5_V = 28672
        O5_VF = 29696
        O5_PROBS = 31744
        ctx5 = A_bf[:, O5_CTX:O5_CTX + 16384].rearrange("p (c j) -> p c j", c=8)
        grid5 = A_bf[:, O5_GRID:O5_GRID + 8192].rearrange("p (c j) -> p c j", c=16)
        wo5_res = A_bf[:, O5_GRID:O5_GRID + 8192].rearrange("p (k c) -> p k c", k=NKC)
        q5 = A_bf[:, O5_Q:O5_Q + 2048]
        k5 = A_bf[:, O5_K:O5_K + 2048]
        v5 = A_bf[:, O5_V:O5_V + 1024].rearrange("p (t c) -> p t c", t=16)
        vf5 = A_bf[:, O5_VF:O5_VF + 2048]
        pr5 = A_bf[:, O5_PROBS:O5_PROBS + 512]
        sc5 = arena_b
        for b in range(BC):
            p.dma("sync", k5[0:64, :], kv_fm_scr[0:HD, b * S:(b + 1) * S], tag="ckl")
            p.dma("sync", vf5[0:64, :], kv_fm_scr[HD:2 * HD, b * S:(b + 1) * S], tag="cvl")
            for tt in range(16):
                bank = ps_bf[0]
                p.op("tensor", lambda e, tt=tt, bank=bank: e.transpose(
                        bank[:, 0:64], vf5[0:64, tt * PT:(tt + 1) * PT], ident_sb[0:64, 0:64]),
                     r=[vf5[0:64, tt * PT:(tt + 1) * PT], ident_sb[0:64, 0:64]],
                     w=[bank[:, 0:64]], tag="cvT")
                p.op("vector", lambda e, tt=tt, bank=bank: e.tensor_copy(
                        v5[:, tt, :], bank[:, 0:64]),
                     r=[bank[:, 0:64]], w=[v5[:, tt, :]], tag="cvTe")
            for h in range(NH):
                p.dma("sync", q5[0:64, :], q_scr[h * HD:(h + 1) * HD, b * S:(b + 1) * S], tag="cql")
                for qs in range(4):
                    for lqt in range(4):
                        qt = qs * 4 + lqt
                        for bi in range(4):
                            bank = ps_f[bi % 2]
                            lhs = q5[0:64, qt * PT:(qt + 1) * PT]
                            rhs = k5[0:64, bi * 512:(bi + 1) * 512]
                            p.op("tensor", lambda e, lhs=lhs, rhs=rhs, bank=bank: e.matmul(
                                    bank[:], lhs, rhs, start=True, stop=True),
                                 r=[lhs, rhs], w=[bank[:]], tag="csc")
                            p.op("vector", lambda e, bank=bank, bi=bi: e.tensor_copy(
                                    sc5[:, bi * 512:(bi + 1) * 512], bank[:]),
                                 r=[bank[:]], w=[sc5[:, bi * 512:(bi + 1) * 512]], tag="cscv")
                        p.op("vector", lambda e: e.reduce_max(
                                stat_sb[:, 0:1], sc5[:, 0:S], axis=AX.X),
                             r=[sc5[:, 0:S]], w=[stat_sb[:, 0:1]], tag="cmx")
                        p.op("vector", lambda e: e.tensor_scalar_mul(stat_sb[:, 1:2],
                                                                     stat_sb[:, 0:1], -1.0),
                             r=[stat_sb[:, 0:1]], w=[stat_sb[:, 1:2]], tag="cnm")
                        for bi in range(4):
                            p.op("scalar", lambda e, bi=bi: e.activation(
                                    sc5[:, bi * 512:(bi + 1) * 512], sc5[:, bi * 512:(bi + 1) * 512],
                                    AF.Exp, bias=stat_sb[:, 1:2],
                                    accum_out=stat_sb[:, 2 + bi:3 + bi]),
                                 r=[sc5[:, bi * 512:(bi + 1) * 512], stat_sb[:, 1:2]],
                                 w=[sc5[:, bi * 512:(bi + 1) * 512], stat_sb[:, 2 + bi:3 + bi]],
                                 tag="cex")
                        for bi in range(1, 4):
                            p.op("vector", lambda e, bi=bi: e.tensor_add(
                                    stat_sb[:, 2:3], stat_sb[:, 2:3], stat_sb[:, 2 + bi:3 + bi]),
                                 r=[stat_sb[:, 2:3], stat_sb[:, 2 + bi:3 + bi]],
                                 w=[stat_sb[:, 2:3]], tag="csa")
                        p.op("vector", lambda e: e.reciprocal(stat_sb[:, 6:7], stat_sb[:, 2:3]),
                             r=[stat_sb[:, 2:3]], w=[stat_sb[:, 6:7]], tag="crc")
                        for kc in range(16):
                            p.op("vector", lambda e, kc=kc: e.tensor_scalar(
                                    pr5[:, 0:PT], sc5[:, kc * PT:(kc + 1) * PT],
                                    stat_sb[:, 6:7], None, ALU.mult),
                                 r=[sc5[:, kc * PT:(kc + 1) * PT], stat_sb[:, 6:7]],
                                 w=[pr5[:, 0:PT]], tag="cnr")
                            bank = ps_bf[kc % 2]
                            p.op("tensor", lambda e, bank=bank: e.transpose(
                                    bank[:, 0:PT], pr5[:, 0:PT], ident_sb[:]),
                                 r=[pr5[:, 0:PT], ident_sb[:]], w=[bank[:, 0:PT]], tag="cpT")
                            p.op("vector", lambda e, kc=kc, lqt=lqt, bank=bank: e.tensor_copy(
                                    grid5[:, kc, lqt * PT:(lqt + 1) * PT], bank[:, 0:PT]),
                                 r=[bank[:, 0:PT]], w=[grid5[:, kc, lqt * PT:(lqt + 1) * PT]],
                                 tag="cpTe")
                    for kc in range(16):
                        lhs = v5[:, kc, :]
                        rhs = grid5[:, kc, :]
                        p.op("tensor", lambda e, lhs=lhs, rhs=rhs,
                                       st=(kc == 0), sp=(kc == 15): e.matmul(
                                ps_f[2][0:64, :], lhs, rhs, start=st, stop=sp),
                             r=[lhs, rhs], w=[ps_f[2][0:64, :]], tag="cctx")
                    hb = (h % 2) * 64
                    p.op("vector", lambda e, hb=hb, h=h, qs=qs: e.tensor_copy(
                            ctx5[hb:hb + 64, h // 2, qs * 512:(qs + 1) * 512], ps_f[2][0:64, :]),
                         r=[ps_f[2][0:64, :]], w=[ctx5[hb:hb + 64, h // 2, qs * 512:(qs + 1) * 512]],
                         tag="cctxe")
            p.dma("sync", wo5_res, w_scr[D:2 * D, :].rearrange("(k p) c -> p k c", p=PT),
                  tag="cwo5")
            for tt in range(16):
                tg = b * 16 + tt
                for half in range(2):
                    bank = ps_f[3 + half]
                    for kc in range(NKC):
                        lhs = ctx5[:, kc, tt * PT:(tt + 1) * PT]
                        p.op("tensor", lambda e, lhs=lhs, kc=kc, half=half, bank=bank: e.matmul(
                                bank[:], lhs, wo5_res[:, kc, half * 512:(half + 1) * 512],
                                start=(kc == 0), stop=(kc == 7)),
                             r=[lhs, wo5_res[:, kc, half * 512:(half + 1) * 512]],
                             w=[bank[:]], tag="co2mm")
                    xsl = x_sb[:, tg * D + half * 512: tg * D + (half + 1) * 512]
                    p.op("vector", lambda e, xsl=xsl, bank=bank: e.tensor_add(xsl, xsl, bank[:]),
                         r=[xsl, bank[:]], w=[xsl], tag="cores")

        if DBG:
            dt_ = nc.dram_tensor("dbg_s5", [T, D], F32, kind="ExternalOutput").ap()
            for t in range(NT):
                p.dma("gpsimd", dt_[t * PT:(t + 1) * PT, :], x_sb[:, t * D:(t + 1) * D], tag="dbg5")

    if N_SUB >= 5:
        sublayer5()

    def sublayer7():
        ein("d_wq", (D, D), prep=lambda i: np.ascontiguousarray(i["L1_attn_b_Wq"].T, np.float32))
        ein("d_wk", (D, D), prep=lambda i: np.ascontiguousarray(i["L1_attn_b_Wk"].T, np.float32))
        ein("d_wv", (D, D), prep=lambda i: np.ascontiguousarray(i["L1_attn_b_Wv"].T, np.float32))
        ein("d_b3", (PT, 3 * NKC), prep=lambda i: np.concatenate(
            [cols128(i["L1_attn_b_bq"]), cols128(i["L1_attn_b_bk"]),
             cols128(i["L1_attn_b_bv"])], 1))
        wq_in = nc_ap("d_wq"); wk_in = nc_ap("d_wk"); wv_in = nc_ap("d_wv")
        w_scr = nc.dram_tensor("d_wscr", [3 * D, D], BF16).ap()
        q_scr = nc.dram_tensor("d_qscr", [D, T], BF16).ap()
        k_scr = nc.dram_tensor("d_kscr", [D, T], BF16).ap()
        vfm_scr = nc.dram_tensor("d_vfm", [D, T], BF16).ap()
        v_scr = nc.dram_tensor("d_vscr", [T, D], BF16).ap()

        stage = arena_b[:, 0:1024]
        stgb = B_bf[:, 2048:3072]
        for wi, (srcw, scale) in enumerate(((wq_in, 1.0 / 32), (wk_in, 1.0), (wv_in, 1.0))):
            for kc in range(NKC):
                p.dma("sync", stage[:, 0:D], srcw[kc * PT:(kc + 1) * PT, :], tag="dwl")
                p.op("vector", lambda e, scale=scale: e.tensor_scalar(
                        stgb[:, 0:D], stage[:, 0:D], float(scale), None, ALU.mult),
                     r=[stage[:, 0:D]], w=[stgb[:, 0:D]], tag="dwc")
                p.dma("gpsimd", w_scr[wi * D + kc * PT: wi * D + (kc + 1) * PT, :],
                      stgb[:, 0:D], tag="dws")
        p.dma("sync", vec_sb[:, 0:24], nc_ap("d_b3"), tag="dbl")
        p.op("vector", lambda e: e.tensor_scalar(vec_sb[:, 0:8], vec_sb[:, 0:8], 1.0 / 32,
                                                 None, ALU.mult),
             r=[vec_sb[:, 0:8]], w=[vec_sb[:, 0:8]], tag="dbq32")

        for t in range(NT):
            ln_tile(t, XT_bf[:])
            fm_ap = A_bf[:].rearrange("p (c j) -> p c j", c=NKC)[:, :, t * PT:(t + 1) * PT]
            transpose_blocks(XT_bf[:], fm_ap, t % 2, NKC, bf=True, tag=f"s7tr{t}")
        xn_fm = A_bf

        wst = XT_bf[:].rearrange("p (k c) -> p k c", k=NKC)
        est = ev_sb.bitcast(BF16)
        for wi, (scr_out, bcol) in enumerate(((q_scr, 0), (k_scr, 8), (vfm_scr, 16))):
            for oc in range(NKC):
                p.dma("sync", wst, w_scr[wi * D:(wi + 1) * D, oc * PT:(oc + 1) * PT].rearrange(
                    "(k p) c -> p k c", p=PT), tag="dwst")
                for ts in range(8):
                    bank = ps_f[ts % 2]
                    for kc in range(NKC):
                        rhs = xn_fm[:, kc * T + ts * 512:kc * T + (ts + 1) * 512]
                        p.op("tensor", lambda e, kc=kc, rhs=rhs, bank=bank: e.matmul(
                                bank[:], wst[:, kc], rhs, start=(kc == 0), stop=(kc == NKC - 1)),
                             r=[wst[:, kc], rhs], w=[bank[:]], tag="dqmm")
                    sl2 = est[:, (ts % 2) * 512:(ts % 2) * 512 + 512]
                    p.op("vector", lambda e, bank=bank, sl2=sl2, oc=oc, bcol=bcol: e.tensor_scalar(
                            sl2, bank[:], vec_sb[:, bcol + oc:bcol + oc + 1], None, ALU.add),
                         r=[bank[:], vec_sb[:, bcol + oc:bcol + oc + 1]], w=[sl2], tag="dqev"),
                    p.dma("gpsimd", scr_out[oc * PT:(oc + 1) * PT, ts * 512:(ts + 1) * 512],
                          sl2, tag="dqst")
        vstg = B_bf[:, 0:1024].rearrange("p (c j) -> p c j", c=NKC)
        for mt in range(NT):
            p.dma("sync", vstg, vfm_scr[:, mt * PT:(mt + 1) * PT].rearrange(
                "(c p) j -> p c j", p=PT), tag="dvfl")
            for g in range(2):
                bank = ps_bf[g]
                for c4 in range(4):
                    cc = g * 4 + c4
                    p.op("tensor", lambda e, cc=cc, c4=c4, bank=bank: e.transpose(
                            bank[:, c4 * PT:(c4 + 1) * PT], vstg[:, cc], ident_sb[:]),
                         r=[vstg[:, cc], ident_sb[:]], w=[bank[:, c4 * PT:(c4 + 1) * PT]],
                         tag="dvT")
                p.op("vector", lambda e, g=g, bank=bank: e.tensor_copy(
                        est[:, g * 512:(g + 1) * 512], bank[:, 0:512]),
                     r=[bank[:, 0:512]], w=[est[:, g * 512:(g + 1) * 512]], tag="dvTe")
            p.dma("gpsimd", v_scr[mt * PT:(mt + 1) * PT, :], est[:, 0:1024], tag="dvTs")

        O7_Q = 0
        O7_K = 4096
        O7_GRID = 20480
        O7_VS = 28672
        O7_PROBS = 29696
        q7 = A_bf[:, O7_Q:O7_Q + 4096].rearrange("p (c j) -> p c j", c=NKC)
        k7 = A_bf[:, O7_K:O7_K + 16384].rearrange("p (c j) -> p c j", c=NKC)
        grid7 = A_bf[:, O7_GRID:O7_GRID + 8192].rearrange("p (c j) -> p c j", c=16)
        vs7 = A_bf[:, O7_VS:O7_VS + 1024]
        pr7 = A_bf[:, O7_PROBS:O7_PROBS + 512]
        sc7 = arena_b
        for b in range(BC):
            p.dma("sync", k7, k_scr[:, b * S:(b + 1) * S].rearrange("(c p) j -> p c j", p=PT),
                  tag="dkl")
            for qs in range(4):
                p.dma("sync", q7, q_scr[:, b * S + qs * 512:b * S + (qs + 1) * 512].rearrange(
                    "(c p) j -> p c j", p=PT), tag="dql")
                for lqt in range(4):
                    qt = qs * 4 + lqt
                    for bi in range(4):
                        bank = ps_f[bi % 2]
                        for kc in range(NKC):
                            lhs = q7[:, kc, lqt * PT:(lqt + 1) * PT]
                            rhs = k7[:, kc, bi * 512:(bi + 1) * 512]
                            p.op("tensor", lambda e, lhs=lhs, rhs=rhs, kc=kc, bank=bank: e.matmul(
                                    bank[:], lhs, rhs, start=(kc == 0), stop=(kc == NKC - 1)),
                                 r=[lhs, rhs], w=[bank[:]], tag="dsc")
                        p.op("vector", lambda e, bank=bank, bi=bi: e.tensor_copy(
                                sc7[:, bi * 512:(bi + 1) * 512], bank[:]),
                             r=[bank[:]], w=[sc7[:, bi * 512:(bi + 1) * 512]], tag="dscv")
                    p.op("vector", lambda e: e.reduce_max(stat_sb[:, 0:1], sc7[:, 0:S], axis=AX.X),
                         r=[sc7[:, 0:S]], w=[stat_sb[:, 0:1]], tag="dmx")
                    p.op("vector", lambda e: e.tensor_scalar_mul(stat_sb[:, 1:2],
                                                                 stat_sb[:, 0:1], -1.0),
                         r=[stat_sb[:, 0:1]], w=[stat_sb[:, 1:2]], tag="dnm")
                    for bi in range(4):
                        p.op("scalar", lambda e, bi=bi: e.activation(
                                sc7[:, bi * 512:(bi + 1) * 512], sc7[:, bi * 512:(bi + 1) * 512],
                                AF.Exp, bias=stat_sb[:, 1:2], accum_out=stat_sb[:, 2 + bi:3 + bi]),
                             r=[sc7[:, bi * 512:(bi + 1) * 512], stat_sb[:, 1:2]],
                             w=[sc7[:, bi * 512:(bi + 1) * 512], stat_sb[:, 2 + bi:3 + bi]],
                             tag="dex")
                    for bi in range(1, 4):
                        p.op("vector", lambda e, bi=bi: e.tensor_add(
                                stat_sb[:, 2:3], stat_sb[:, 2:3], stat_sb[:, 2 + bi:3 + bi]),
                             r=[stat_sb[:, 2:3], stat_sb[:, 2 + bi:3 + bi]],
                             w=[stat_sb[:, 2:3]], tag="dsa")
                    p.op("vector", lambda e: e.reciprocal(stat_sb[:, 6:7], stat_sb[:, 2:3]),
                         r=[stat_sb[:, 2:3]], w=[stat_sb[:, 6:7]], tag="drc")
                    for kc in range(16):
                        p.op("vector", lambda e, kc=kc: e.tensor_scalar(
                                pr7[:, 0:PT], sc7[:, kc * PT:(kc + 1) * PT],
                                stat_sb[:, 6:7], None, ALU.mult),
                             r=[sc7[:, kc * PT:(kc + 1) * PT], stat_sb[:, 6:7]],
                             w=[pr7[:, 0:PT]], tag="dnr")
                        bank = ps_bf[kc % 2]
                        p.op("tensor", lambda e, bank=bank: e.transpose(
                                bank[:, 0:PT], pr7[:, 0:PT], ident_sb[:]),
                             r=[pr7[:, 0:PT], ident_sb[:]], w=[bank[:, 0:PT]], tag="dpT")
                        p.op("vector", lambda e, kc=kc, lqt=lqt, bank=bank: e.tensor_copy(
                                grid7[:, kc, lqt * PT:(lqt + 1) * PT], bank[:, 0:PT]),
                             r=[bank[:, 0:PT]], w=[grid7[:, kc, lqt * PT:(lqt + 1) * PT]],
                             tag="dpTe")
                for half in range(2):
                    for mc in range(16):
                        vsl = vs7[:, (mc % 2) * 512:(mc % 2) * 512 + 512]
                        p.dma("sync", vsl, v_scr[b * S + mc * PT:b * S + (mc + 1) * PT,
                                                 half * 512:(half + 1) * 512], tag="dvl2")
                        for lqt in range(4):
                            bank = ps_f[2 + lqt]
                            lhs = grid7[:, mc, lqt * PT:(lqt + 1) * PT]
                            p.op("tensor", lambda e, lhs=lhs, vsl=vsl, mc=mc, bank=bank: e.matmul(
                                    bank[:], lhs, vsl, start=(mc == 0), stop=(mc == 15)),
                                 r=[lhs, vsl], w=[bank[:]], tag="dzmm")
                    for lqt in range(4):
                        qt = qs * 4 + lqt
                        bank = ps_f[2 + lqt]
                        tg = b * 16 + qt
                        xsl = x_sb[:, tg * D + half * 512: tg * D + (half + 1) * 512]
                        p.op("vector", lambda e, xsl=xsl, bank=bank: e.tensor_add(xsl, xsl, bank[:]),
                             r=[xsl, bank[:]], w=[xsl], tag="dzres")

        if DBG:
            dt_ = nc.dram_tensor("dbg_s7", [T, D], F32, kind="ExternalOutput").ap()
            for t in range(NT):
                p.dma("gpsimd", dt_[t * PT:(t + 1) * PT, :], x_sb[:, t * D:(t + 1) * D], tag="dbg7")

    CLOC = 1024
    NEXP = 8
    moe_state = {}

    def sublayer_moe(app):
        sfx = f"m{app}"
        if app == 0:
            ein("m_rw", (D, NEXP), prep=lambda i: np.ascontiguousarray(
                i["L1_ffn_router_w"], np.float32))
            ein("m_w1", (NEXP, D, 4 * D), prep=lambda i: np.ascontiguousarray(
                i["L1_ffn_w1"], np.float32))
            ein("m_w2", (NEXP, 4 * D, D), prep=lambda i: np.ascontiguousarray(
                i["L1_ffn_w2"], np.float32))
            ein("m_bias", (1, D), prep=lambda i: np.ascontiguousarray(
                i["L1_ffn_bias"].reshape(1, D), np.float32))
            moe_state["w1"] = nc.dram_tensor("m_w1scr", [NEXP * 32 * NKC * PT, PT], BF16).ap()
            moe_state["w2"] = nc.dram_tensor("m_w2scr", [NEXP * 4 * D, D], BF16).ap()
            stage4 = arena_b[:, 0:1024]
            stg4b = B_bf[:, 2048:4096]
            for eidx in range(NEXP):
                p.dma("gpsimd", moe_state["w2"][eidx * 4 * D:(eidx + 1) * 4 * D, :],
                      nc_ap("m_w2")[eidx], tag="mw2c")
                for kc in range(NKC):
                    for qu in range(4):
                        p.dma("sync", stage4[:, 0:1024],
                              nc_ap("m_w1")[eidx, kc * PT:(kc + 1) * PT,
                                            qu * 1024:(qu + 1) * 1024], tag="mw1l")
                        p.op("vector", lambda e: e.tensor_copy(stg4b[:, 0:1024], stage4[:, 0:1024]),
                             r=[stage4[:, 0:1024]], w=[stg4b[:, 0:1024]], tag="mw1c")
                        dst = moe_state["w1"].rearrange(
                            "(e h k p) c -> e k p h c", e=NEXP, h=32, k=NKC, p=PT)[
                            eidx, kc, :, qu * 8:(qu + 1) * 8, :]
                        p.dma("gpsimd", dst,
                              stg4b[:, 0:1024].rearrange("p (h c) -> p h c", h=8), tag="mw1s")
        w1_scr = moe_state["w1"]
        w2_scr = moe_state["w2"]

        xn_scr = nc.dram_tensor(f"m_xnscr_{sfx}", [T, D], BF16).ap()
        xe_scr = nc.dram_tensor(f"m_xescr_{sfx}", [16 * CLOC + PT, D], BF16).ap()
        ye_scr = nc.dram_tensor(f"m_yescr_{sfx}", [16 * CLOC + PT, D], BF16).ap()
        cc_in = nc.dram_tensor(f"m_ccin_{sfx}", [1, 16], F32).ap().opt()
        cc_out = nc.dram_tensor(f"m_ccout_{sfx}", [8, 16], F32).ap().opt()

        rt = arena_b[:, 0:128]
        iota8 = arena_b[:, 128:136]
        eight8 = arena_b[:, 136:144]
        neg18 = arena_b[:, 144:152]
        wk8 = arena_b[:, 152:176]
        offs_tok = arena_b[:, 176:192]
        base_tok = arena_b[:, 192:208]
        cnts_all = arena_b[:, 208:224]
        mrow = arena_b[:, 224:240]
        slots_i = arena_b.bitcast(I32)[:, 272:336]
        equ8 = arena_b.bitcast(U8)[:, 1344:1376]
        ybuf = B_bf[:, 768:1792]
        bias_bc = B_bf[:, 1792:2816]
        xstage = B_bf[:, 2816:3840]
        rw_sb = B_bf[:, 3840:3904].rearrange("p (k c) -> p k c", k=NKC)

        p.op("gpsimd", lambda e: e.iota(iota8, [[1, 8]], base=0, channel_multiplier=0,
                                        allow_small_or_imprecise_dtypes=True),
             w=[iota8], tag="miota")
        p.op("vector", lambda e: e.memset(eight8, 8.0), w=[eight8], tag="mc8")
        p.op("vector", lambda e: e.memset(neg18, -1.0), w=[neg18], tag="mcn1")
        p.dma("gpsimd", rw_sb, nc_ap("m_rw").rearrange("(k p) c -> p k c", p=PT), tag="mrwl")
        for qq in range(4):
            bank = ps_f[5]
            srcrow = arena_b[0:1, 1664:1920]
            p.dma("sync", srcrow, nc_ap("m_bias")[0:1, qq * 256:(qq + 1) * 256], tag="mbl")
            p.op("vector", lambda e, srcrow=srcrow: e.tensor_copy(B_bf[0:1, 3584:3584 + 256],
                                                                  srcrow),
                 r=[srcrow], w=[B_bf[0:1, 3584:3584 + 256]], tag="mbc0")
            p.op("tensor", lambda e, bank=bank: e.matmul(bank[:, 0:256], ones_bf[0:1, 0:PT],
                                              B_bf[0:1, 3584:3584 + 256], start=True, stop=True),
                 r=[ones_bf[0:1, 0:PT], B_bf[0:1, 3584:3584 + 256]], w=[bank[:, 0:256]], tag="mbcm")
            p.op("vector", lambda e, qq=qq, bank=bank: e.tensor_copy(
                    bias_bc[:, qq * 256:(qq + 1) * 256], bank[:, 0:256]),
                 r=[bank[:, 0:256]], w=[bias_bc[:, qq * 256:(qq + 1) * 256]], tag="mbce")

        for t in range(NT):
            ln_tile(t, XT_bf[:])
            p.dma("gpsimd", xn_scr[t * PT:(t + 1) * PT, :], XT_bf[:], tag="mxn")
            fm_ap = A_bf[:].rearrange("p (c j) -> p c j", c=NKC)[:, :, t * PT:(t + 1) * PT]
            transpose_blocks(XT_bf[:], fm_ap, t % 2, NKC, bf=True, tag=f"mtr{t}")
        xn_fm = A_bf

        for t in range(NT):
            bank = ps_f[4]
            for kc in range(NKC):
                lhs = xn_fm[:, kc * T + t * PT:kc * T + (t + 1) * PT]
                p.op("tensor", lambda e, lhs=lhs, kc=kc, bank=bank: e.matmul(
                        bank[:, 0:NEXP], lhs, rw_sb[:, kc], start=(kc == 0), stop=(kc == NKC - 1)),
                     r=[lhs, rw_sb[:, kc]], w=[bank[:, 0:NEXP]], tag="mrt")
            sc8 = wk8[:, 0:8]
            p.op("vector", lambda e, bank=bank: e.tensor_copy(sc8, bank[:, 0:NEXP]),
                 r=[bank[:, 0:NEXP]], w=[sc8], tag="mrte")
            p.op("vector", lambda e: e.reduce_max(stat_sb[:, 0:1], sc8, axis=AX.X),
                 r=[sc8], w=[stat_sb[:, 0:1]], tag="mrm")
            p.op("vector", lambda e: e.tensor_scalar_mul(stat_sb[:, 1:2], stat_sb[:, 0:1], -1.0),
                 r=[stat_sb[:, 0:1]], w=[stat_sb[:, 1:2]], tag="mrn")
            p.op("scalar", lambda e: e.activation(sc8, sc8, AF.Exp, bias=stat_sb[:, 1:2],
                                                  accum_out=stat_sb[:, 2:3]),
                 r=[sc8, stat_sb[:, 1:2]], w=[sc8, stat_sb[:, 2:3]], tag="mre")
            p.op("vector", lambda e: e.reciprocal(stat_sb[:, 3:4], stat_sb[:, 2:3]),
                 r=[stat_sb[:, 2:3]], w=[stat_sb[:, 3:4]], tag="mrr")
            p.op("vector", lambda e: e.tensor_scalar(sc8, sc8, stat_sb[:, 3:4], None, ALU.mult),
                 r=[sc8, stat_sb[:, 3:4]], w=[sc8], tag="mrp")
            m1 = rt[:, t * 4 + 2:t * 4 + 3]
            m2 = rt[:, t * 4 + 3:t * 4 + 4]
            i1 = rt[:, t * 4 + 0:t * 4 + 1]
            i2 = rt[:, t * 4 + 1:t * 4 + 2]
            eq = equ8[:, 0:8]
            sel = wk8[:, 8:16]
            msk = wk8[:, 16:24]
            p.op("vector", lambda e, m1=m1: e.reduce_max(m1, sc8, axis=AX.X),
                 r=[sc8], w=[m1], tag="mt1")
            p.op("vector", lambda e, m1=m1: e.tensor_scalar(eq, sc8, m1, None, ALU.is_ge),
                 r=[sc8, m1], w=[eq], tag="mt2")
            p.op("vector", lambda e: e.select(sel, eq, iota8, eight8, add_drain=True),
                 r=[eq, iota8, eight8], w=[sel], tag="mt3")
            p.op("vector", lambda e, i1=i1: e.tensor_reduce(i1, sel, AX.X, ALU.min),
                 r=[sel], w=[i1], tag="mt4")
            p.op("vector", lambda e: e.select(msk, eq, neg18, sc8, add_drain=True),
                 r=[eq, neg18, sc8], w=[msk], tag="mt5")
            p.op("vector", lambda e, m2=m2: e.reduce_max(m2, msk, axis=AX.X),
                 r=[msk], w=[m2], tag="mt6")
            p.op("vector", lambda e, m2=m2: e.tensor_scalar(eq, msk, m2, None, ALU.is_ge),
                 r=[msk, m2], w=[eq], tag="mt7")
            p.op("vector", lambda e: e.select(sel, eq, iota8, eight8, add_drain=True),
                 r=[eq, iota8, eight8], w=[sel], tag="mt8")
            p.op("vector", lambda e, i2=i2: e.tensor_reduce(i2, sel, AX.X, ALU.min),
                 r=[sel], w=[i2], tag="mt9")

        ind = A_f[:, 0:4096]
        onesA = A_f[:, 4096:8192]
        cnti = A_f[:, 8192:12288]
        itok = A_f[:, 12288:12304]
        ctok = A_f[:, 12304:12320]
        prodA = A_f[:, 12320:12336]
        p.op("vector", lambda e: e.memset(onesA[0:16, :], 1.0), w=[onesA[0:16, :]], tag="mones")
        for t in range(NT):
            for k in range(2):
                idx = rt[:, t * 4 + k:t * 4 + k + 1]
                p.op("vector", lambda e, idx=idx, k=k: e.tensor_tensor(
                        itok[:, k * 8:(k + 1) * 8], idx.broadcast_to((PT, 8)), iota8, ALU.is_equal),
                     r=[idx, iota8], w=[itok[:, k * 8:(k + 1) * 8]], tag="mind")
            bank = ps_f[4]
            p.op("tensor", lambda e, t=t, bank=bank: e.transpose(
                    bank[0:16, 0:PT], itok, identf_sb[:]),
                 r=[itok, identf_sb[:]], w=[bank[0:16, 0:PT]], tag="mindT")
            p.op("vector", lambda e, t=t, bank=bank: e.tensor_copy(
                    ind[0:16, t * PT:(t + 1) * PT], bank[0:16, 0:PT].transpose([0, 1]) if False
                    else bank[0:16, 0:PT]),
                 r=[bank[0:16, 0:PT]], w=[ind[0:16, t * PT:(t + 1) * PT]], tag="mindE")
        p.op("vector", lambda e: e.tensor_tensor_scan(cnti[0:16, :], onesA[0:16, :], ind[0:16, :],
                                                      0.0, ALU.mult, ALU.add),
             r=[onesA[0:16, :], ind[0:16, :]], w=[cnti[0:16, :]], tag="mscan")

        bank = ps_f[4]
        p.op("tensor", lambda e, bank=bank: e.transpose(bank[0:1, 0:16], cnti[0:16, 4095:4096],
                                                        identf_sb[0:16, 0:16]),
             r=[cnti[0:16, 4095:4096], identf_sb[0:16, 0:16]], w=[bank[0:1, 0:16]], tag="mcntT")
        p.op("vector", lambda e, bank=bank: e.tensor_copy(mrow[0:1, 0:16], bank[0:1, 0:16]),
             r=[bank[0:1, 0:16]], w=[mrow[0:1, 0:16]], tag="mcntE")
        p.dma("sync", cc_in, mrow[0:1, 0:16], tag="mccw")
        if SIM_MODE:
            p.op("vector", lambda e: e.memset(cnts_all[0:8, :], 0.0),
                 w=[cnts_all[0:8, :]], tag="mccz")
            p.op("vector", lambda e: e.tensor_copy(cnts_all[0:1, 0:16], mrow[0:1, 0:16]),
                 r=[mrow[0:1, 0:16]], w=[cnts_all[0:1, 0:16]], tag="mccc")
        else:
            p.op("gpsimd", lambda e: e.collective_compute(
                    "AllGather", ALU.bypass, replica_groups=[list(range(8))],
                    ins=[cc_in], outs=[cc_out]),
                 r=[cc_in], w=[cc_out], cc=True, tag="mcc")
            p.dma("sync", cnts_all[0:8, :], cc_out, tag="mccr")
        offrow = arena_b[:, 244:260]
        if SIM_MODE:
            p.op("vector", lambda e: e.memset(offrow[0:1, 0:16], 0.0),
                 w=[offrow[0:1, 0:16]], tag="moffz")
        else:
            pid_i = arena_b.bitcast(I32)[0:1, 336:337]
            pid_sb = stat_sb[0:1, 4:5]
            p.op("vector", lambda e: e.store(pid_i, e.partition_id()), w=[pid_i], tag="mpid")
            p.op("vector", lambda e: e.tensor_copy(pid_sb, pid_i),
                 r=[pid_i], w=[pid_sb], tag="mpidc")
            ones8r = arena_b[:, 276:284]
            p.op("vector", lambda e: e.memset(ones8r[0:1, 0:8], 1.0),
                 w=[ones8r[0:1, 0:8]], tag="mo8")
            bank = ps_f[4]
            p.op("tensor", lambda e, bank=bank: e.transpose(bank[0:8, 0:1], iota8[0:1, 0:8],
                                                            identf_sb[0:1, 0:1]),
                 r=[iota8[0:1, 0:8], identf_sb[0:1, 0:1]], w=[bank[0:8, 0:1]], tag="miotaT")
            iocol = arena_b[:, 284:285]
            p.op("vector", lambda e, bank=bank: e.tensor_copy(iocol[0:8, 0:1], bank[0:8, 0:1]),
                 r=[bank[0:8, 0:1]], w=[iocol[0:8, 0:1]], tag="miotaE")
            pidb = arena_b[:, 285:286]
            p.op("tensor", lambda e, bank=bank: e.matmul(bank[0:8, 1:2], ones8r[0:1, 0:8],
                                                         pid_sb, start=True, stop=True),
                 r=[ones8r[0:1, 0:8], pid_sb], w=[bank[0:8, 1:2]], tag="mpidB")
            p.op("vector", lambda e, bank=bank: e.tensor_copy(pidb[0:8, 0:1], bank[0:8, 1:2]),
                 r=[bank[0:8, 1:2]], w=[pidb[0:8, 0:1]], tag="mpidE")
            maskc = arena_b[:, 286:287]
            p.op("vector", lambda e: e.tensor_tensor(maskc[0:8, 0:1], iocol[0:8, 0:1],
                                                     pidb[0:8, 0:1], ALU.is_lt),
                 r=[iocol[0:8, 0:1], pidb[0:8, 0:1]], w=[maskc[0:8, 0:1]], tag="mmask")
            p.op("tensor", lambda e, bank=bank: e.matmul(bank[0:1, 16:32], maskc[0:8, 0:1],
                                                         cnts_all[0:8, 0:16], start=True, stop=True),
                 r=[maskc[0:8, 0:1], cnts_all[0:8, 0:16]], w=[bank[0:1, 16:32]], tag="moff")
            p.op("vector", lambda e, bank=bank: e.tensor_copy(offrow[0:1, 0:16], bank[0:1, 16:32]),
                 r=[bank[0:1, 16:32]], w=[offrow[0:1, 0:16]], tag="moffE")
        baserow = arena_b[:, 260:276]
        p.op("gpsimd", lambda e: e.iota(baserow[0:1, 0:16], [[1, 16]], base=0, channel_multiplier=0,
                                        allow_small_or_imprecise_dtypes=True),
             w=[baserow[0:1, 0:16]], tag="mbase")
        p.op("vector", lambda e: e.tensor_scalar_mul(baserow[0:1, 0:16], baserow[0:1, 0:16],
                                                     float(CLOC)),
             r=[baserow[0:1, 0:16]], w=[baserow[0:1, 0:16]], tag="mbase2")
        onescol = arena_b[:, 287:415]
        p.op("vector", lambda e: e.memset(onescol[0:1, 0:PT], 1.0),
             w=[onescol[0:1, 0:PT]], tag="mocol")
        bank = ps_f[4]
        for (row, dstb, pcol) in ((offrow, offs_tok, 32), (baserow, base_tok, 48)):
            p.op("tensor", lambda e, row=row, pcol=pcol, bank=bank: e.matmul(
                    bank[:, pcol:pcol + 16], onescol[0:1, 0:PT], row[0:1, 0:16],
                    start=True, stop=True),
                 r=[onescol[0:1, 0:PT], row[0:1, 0:16]], w=[bank[:, pcol:pcol + 16]], tag="mbc")
            p.op("vector", lambda e, dstb=dstb, pcol=pcol, bank=bank: e.tensor_copy(
                    dstb, bank[:, pcol:pcol + 16]),
                 r=[bank[:, pcol:pcol + 16]], w=[dstb], tag="mbcE")

        p.op("vector", lambda e: e.memset(xstage, 0.0), w=[xstage], tag="mz0")
        for blk in range(16 * CLOC // PT):
            p.dma("gpsimd", xe_scr[blk * PT:(blk + 1) * PT, :], xstage, tag="mzx")
        p.dma("gpsimd", ye_scr[16 * CLOC:16 * CLOC + PT, :], xstage, tag="mzy")
        for t in range(NT):
            for k in range(2):
                idx = rt[:, t * 4 + k:t * 4 + k + 1]
                p.op("vector", lambda e, idx=idx, k=k: e.tensor_tensor(
                        itok[:, k * 8:(k + 1) * 8], idx.broadcast_to((PT, 8)), iota8, ALU.is_equal),
                     r=[idx, iota8], w=[itok[:, k * 8:(k + 1) * 8]], tag="mind2")
            bank = ps_f[4]
            p.op("tensor", lambda e, t=t, bank=bank: e.transpose(
                    bank[:, 0:16], cnti[0:16, t * PT:(t + 1) * PT], identf_sb[0:16, 0:16]),
                 r=[cnti[0:16, t * PT:(t + 1) * PT], identf_sb[0:16, 0:16]],
                 w=[bank[:, 0:16]], tag="mcT")
            p.op("vector", lambda e, bank=bank: e.tensor_copy(ctok, bank[:, 0:16]),
                 r=[bank[:, 0:16]], w=[ctok], tag="mcTe")
            for (src, dcol) in ((ctok, 6), (offs_tok, 8), (base_tok, 10)):
                p.op("vector", lambda e, src=src: e.tensor_mul(prodA, itok, src),
                     r=[itok, src], w=[prodA], tag="mpr")
                p.op("vector", lambda e, dcol=dcol: e.tensor_reduce(
                        stat_sb[:, dcol:dcol + 2], prodA.rearrange("p (k c) -> p k c", k=2),
                        AX.X, ALU.add),
                     r=[prodA], w=[stat_sb[:, dcol:dcol + 2]], tag="mrd")
            for k in range(2):
                locx = stat_sb[:, 6 + k:7 + k]
                offk = stat_sb[:, 8 + k:9 + k]
                bask = stat_sb[:, 10 + k:11 + k]
                p.op("vector", lambda e, locx=locx: e.tensor_scalar_add(stat_sb[:, 12:13], locx, -1.0),
                     r=[locx], w=[stat_sb[:, 12:13]], tag="ms1")
                p.op("vector", lambda e, offk=offk: e.tensor_add(stat_sb[:, 13:14],
                                                                 stat_sb[:, 12:13], offk),
                     r=[stat_sb[:, 12:13], offk], w=[stat_sb[:, 13:14]], tag="ms2")
                p.op("vector", lambda e: e.tensor_single_scalar(stat_sb[:, 13:14],
                                                                stat_sb[:, 13:14], 4096.0, ALU.is_ge),
                     r=[stat_sb[:, 13:14]], w=[stat_sb[:, 13:14]], tag="ms3")
                p.op("vector", lambda e: e.tensor_single_scalar(wk8[:, 0:1], stat_sb[:, 12:13],
                                                                float(CLOC), ALU.is_ge),
                     r=[stat_sb[:, 12:13]], w=[wk8[:, 0:1]], tag="ms4")
                p.op("vector", lambda e, bask=bask: e.tensor_add(stat_sb[:, 12:13],
                                                                 stat_sb[:, 12:13], bask),
                     r=[stat_sb[:, 12:13], bask], w=[stat_sb[:, 12:13]], tag="ms5")
                p.op("vector", lambda e: e.tensor_add(stat_sb[:, 13:14], stat_sb[:, 13:14],
                                                      wk8[:, 0:1]),
                     r=[stat_sb[:, 13:14], wk8[:, 0:1]], w=[stat_sb[:, 13:14]], tag="ms6")
                p.op("vector", lambda e: e.scalar_tensor_tensor(
                        stat_sb[:, 12:13], stat_sb[:, 13:14], float(16 * CLOC + PT),
                        stat_sb[:, 12:13], ALU.mult, ALU.add),
                     r=[stat_sb[:, 13:14], stat_sb[:, 12:13]], w=[stat_sb[:, 12:13]], tag="ms7")
                p.op("vector", lambda e: e.tensor_scalar_min(stat_sb[:, 12:13], stat_sb[:, 12:13],
                                                             float(16 * CLOC)),
                     r=[stat_sb[:, 12:13]], w=[stat_sb[:, 12:13]], tag="ms8")
                p.op("vector", lambda e, t=t, k=k: e.tensor_copy(
                        slots_i[:, t * 2 + k:t * 2 + k + 1], stat_sb[:, 12:13]),
                     r=[stat_sb[:, 12:13]], w=[slots_i[:, t * 2 + k:t * 2 + k + 1]], tag="ms9")
            p.dma("sync", xstage, xn_scr[t * PT:(t + 1) * PT, :], tag="mxl")
            for k in range(2):
                p.op("gpsimd", lambda e, t=t, k=k: e.indirect_dma_start(
                        xe_scr[:, :], IndirectOffsetOnAxis(slots_i[:, t * 2 + k:t * 2 + k + 1], 0),
                        xstage, None),
                     r=[xstage, slots_i[:, t * 2 + k:t * 2 + k + 1]], w=[xe_scr[:, :]],
                     dma=True, tag="mscat")

        xe_fm = A_bf[:, 0:8192].rearrange("p (c j) -> p c j", c=NKC)
        hh_m = A_bf[:, 8192:8192 + 32 * 512].rearrange("p (c j) -> p c j", c=32)
        w1buf = A_bf[:, 24576:24576 + 4096].rearrange("p (b k c) -> p b k c", b=4, k=NKC)
        w2buf = A_bf[:, 28672:28672 + 2048].rearrange("p (b c) -> p b c", b=4)
        gsl = [ev_sb[:, 0:512], sq_f[:, 0:512], A_f[:, 15360:15872], A_f[:, 15872:16384]]
        for ek in range(16):
            eidx = ek % 8
            for st in range(CLOC // PT):
                p.dma("sync", xstage, xe_scr[ek * CLOC + st * PT: ek * CLOC + (st + 1) * PT, :],
                      tag="mxel")
                for g in range(2):
                    bank = ps_bf[g]
                    for c4 in range(4):
                        cc = g * 4 + c4
                        p.op("tensor", lambda e, cc=cc, c4=c4, bank=bank: e.transpose(
                                bank[:, c4 * PT:(c4 + 1) * PT], xstage[:, cc * PT:(cc + 1) * PT],
                                ident_sb[:]),
                             r=[xstage[:, cc * PT:(cc + 1) * PT], ident_sb[:]],
                             w=[bank[:, c4 * PT:(c4 + 1) * PT]], tag="mxeT")
                    dst = xe_fm[:, g * 4:(g + 1) * 4, st * PT:(st + 1) * PT]
                    p.op("vector", lambda e, bank=bank, dst=dst: e.tensor_copy(
                            dst, bank[:, 0:512].rearrange("p (c j) -> p c j", c=4)),
                         r=[bank[:, 0:512]], w=[dst], tag="mxeTe")
            for half in range(CLOC // 512):
                ssl = slice(half * 512, (half + 1) * 512)
                for hc in range(32):
                    wb = w1buf[:, hc % 4]
                    p.dma("sync", wb, w1_scr[(eidx * 32 + hc) * NKC * PT:
                                             (eidx * 32 + hc + 1) * NKC * PT, :].rearrange(
                        "(k p) c -> p k c", p=PT), tag="mw1st")
                    b1 = ps_f[hc % 4]
                    for kc in range(NKC):
                        p.op("tensor", lambda e, kc=kc, b1=b1, ssl=ssl, wb=wb: e.matmul(
                                b1[:], wb[:, kc], xe_fm[:, kc, ssl],
                                start=(kc == 0), stop=(kc == NKC - 1)),
                             r=[wb[:, kc], xe_fm[:, kc, ssl]], w=[b1[:]], tag="mh1")
                    g = gsl[hc % 4]
                    p.op("scalar", lambda e, b1=b1, g=g: e.activation(g, b1[:], AF.Square),
                         r=[b1[:]], w=[g], tag="mg1")
                    p.op("vector", lambda e, g=g: e.tensor_scalar(g, g,
                                                                  0.044715, 1.0, ALU.mult, ALU.add),
                         r=[g], w=[g], tag="mg2")
                    p.op("vector", lambda e, b1=b1, g=g: e.tensor_mul(g, g, b1[:]),
                         r=[g, b1[:]], w=[g], tag="mg3")
                    p.op("scalar", lambda e, g=g: e.activation(g, g,
                                                          AF.Sigmoid, scale=1.5957691216057308),
                         r=[g], w=[g], tag="mg4")
                    p.op("vector", lambda e, hc=hc, b1=b1, g=g: e.tensor_mul(
                            hh_m[:, hc, 0:512], g, b1[:]),
                         r=[g, b1[:]], w=[hh_m[:, hc, 0:512]], tag="mg5")
                for oh in range(2):
                    for hc in range(32):
                        w2b = w2buf[:, hc % 4]
                        p.dma("sync", w2b,
                              w2_scr[eidx * 4 * D + hc * PT: eidx * 4 * D + (hc + 1) * PT,
                                     oh * 512:(oh + 1) * 512], tag="mw2st")
                        for sl4 in range(4):
                            bank = ps_f[2 + sl4]
                            p.op("tensor", lambda e, hc=hc, sl4=sl4, bank=bank, w2b=w2b: e.matmul(
                                    bank[:], hh_m[:, hc, sl4 * PT:(sl4 + 1) * PT], w2b,
                                    start=(hc == 0), stop=(hc == 31)),
                                 r=[hh_m[:, hc, sl4 * PT:(sl4 + 1) * PT], w2b],
                                 w=[bank[:]], tag="my2")
                    for sl4 in range(4):
                        bank = ps_f[2 + sl4]
                        xsl2 = xstage[:, (sl4 % 2) * 512:(sl4 % 2 + 1) * 512]
                        p.op("vector", lambda e, bank=bank, xsl2=xsl2: e.tensor_copy(
                                xsl2, bank[:]),
                             r=[bank[:]], w=[xsl2], tag="my2e")
                        p.dma("gpsimd",
                              ye_scr[ek * CLOC + half * 512 + sl4 * PT:
                                     ek * CLOC + half * 512 + (sl4 + 1) * PT,
                                     oh * 512:(oh + 1) * 512], xsl2, tag="my2s")

        for t in range(NT):
            xt = x_sb[:, t * D:(t + 1) * D]
            for k in range(2):
                p.op("vector", lambda e: e.memset(ybuf, 0.0), w=[ybuf], tag="myz")
                p.op("gpsimd", lambda e, t=t, k=k: e.indirect_dma_start(
                        ybuf, None, ye_scr[:, :],
                        IndirectOffsetOnAxis(slots_i[:, t * 2 + k:t * 2 + k + 1], 0)),
                     r=[ye_scr[:, :], slots_i[:, t * 2 + k:t * 2 + k + 1]], w=[ybuf],
                     dma=True, tag="mgat")
                wcol = rt[:, t * 4 + 2 + k:t * 4 + 3 + k]
                p.op("vector", lambda e, xt=xt, wcol=wcol: e.scalar_tensor_tensor(
                        xt, ybuf, wcol, xt, ALU.mult, ALU.add),
                     r=[ybuf, wcol, xt], w=[xt], tag="mcomb")
            p.op("vector", lambda e, xt=xt: e.tensor_tensor(xt, xt, bias_bc, ALU.add),
                 r=[xt, bias_bc], w=[xt], tag="mbias")

        if DBG:
            dt_ = nc.dram_tensor(f"dbg_{sfx}", [T, D], F32, kind="ExternalOutput").ap()
            for t in range(NT):
                p.dma("gpsimd", dt_[t * PT:(t + 1) * PT, :], x_sb[:, t * D:(t + 1) * D],
                      tag="dbgm")

    if N_SUB >= 6:
        sublayer_moe(0)
    if N_SUB >= 7:
        sublayer7()
    if N_SUB >= 8:
        sublayer_moe(1)

    for t in range(NT):
        p.dma("sync", out_t[t * PT:(t + 1) * PT, :], x_sb[:, t * D:(t + 1) * D], tag="store")
    p.op("sync", lambda e: None, r=[out_t[:, :]], tag="final")

    p.emit()
    return nc, ext, PREP


_CACHE = {}


def _get_program():
    if N_SUB not in _CACHE:
        _CACHE[N_SUB] = build_program()
    return _CACHE[N_SUB]


def make_in_maps(nc_ext_prep, inputs):
    nc, ext, prep = nc_ext_prep
    x = np.ascontiguousarray(inputs["L0_attn_a_x"], np.float32)
    common = {}
    for name in ext:
        if name == "x":
            continue
        common[name] = np.ascontiguousarray(prep[name](inputs))
    in_maps = []
    for c in range(8):
        m = dict(common)
        m["x"] = np.ascontiguousarray(x[2 * c:2 * c + 2].reshape(T, D))
        in_maps.append(m)
    for m in in_maps:
        assert set(m.keys()) == set(ext.keys()), (set(ext) - set(m), set(m) - set(ext))
        for k in m:
            assert tuple(m[k].shape) == tuple(ext[k][0]), (k, m[k].shape, ext[k][0])
    return in_maps


def kernel(**inputs):
    prog = _get_program()
    in_maps = make_in_maps(prog, inputs)
    res = run_bass_kernel_spmd(prog[0], in_maps, core_ids=list(range(8)))
    return np.concatenate([r["out"].reshape(2, S, D) for r in res.results], axis=0)
